# version 02
# baseline (speedup 1.0000x reference)
import numpy as np
from contextlib import ExitStack
import concourse.bass as bass
import concourse.mybir as mybir
from concourse.bass import IndirectOffsetOnAxis
from concourse.bass_utils import run_bass_kernel_spmd

F32 = mybir.dt.float32
BF16 = mybir.dt.bfloat16
I32 = mybir.dt.int32
U8 = mybir.dt.uint8
AF = mybir.ActivationFunctionType
ALU = mybir.AluOpType
AXX = mybir.AxisListType.X

T = 2048
NT = 16
D = 1024
DC = 8
E = 8
SLOT = 512
RT = SLOT // 128
NH = 16
DH = 64
HF = 4096
NHC = HF // 128
EPS = 1e-5
CAP = 3072.0
BIS_IT = 26
REC = 1.0 / D
NCHUNK = 16
RG8 = [[0, 1, 2, 3, 4, 5, 6, 7]]

DO_ATTN = True
DO_FFN2 = True


class _MockIns:
    __slots__ = ()

    def then_inc(self, *a, **k):
        return self


_MOCK_INS = _MockIns()


class _MockEng:
    def __getattr__(self, name):
        def f(*a, **k):
            return _MOCK_INS
        return f


_MOCK = _MockEng()


class Sem:
    __slots__ = ("h", "n", "w", "name")

    def __init__(self, h, name):
        self.h = h
        self.n = 0
        self.w = {}
        self.name = name

    def snap(self):
        return (self.n, dict(self.w))

    def restore(self, s):
        self.n, self.w = s[0], dict(s[1])


class Ctx:
    pass


def sec(c, ename):
    def deco(fn):
        if c.dry:
            fn(_MOCK)
        else:
            getattr(c.block, ename)(fn)
        return None
    return deco


def inc(c, ins, s, amt, event=None):
    s.n += amt
    if event is not None:
        if c.dry:
            c.events[event] = (s, s.n)
        else:
            assert c.events[event][1] == s.n, (event, c.events[event][1], s.n)
    if not c.dry:
        ins.then_inc(s.h, amt)
    return s.n


def cc_inc(c, ins, event):
    s = c.sems["ccs"]
    s.n += 1
    if c.dry:
        c.events[event] = (s, s.n)
    else:
        ins.then_inc(s.h)
    return s.n


def wait_ev(c, eng, ename, event):
    if c.dry:
        return
    s, v = c.events[event]
    if v <= 0 or v <= s.w.get(ename, 0):
        return
    eng.wait_ge(s.h, v)
    s.w[ename] = v


def vf(c, eng, ename, s, ins):
    v = inc(c, ins, s, 1)
    wait_abs(c, eng, ename, s, v)


def wait_abs(c, eng, ename, s, v):
    if c.dry:
        return
    if v <= 0 or v <= s.w.get(ename, 0):
        return
    eng.wait_ge(s.h, v)
    s.w[ename] = v


def run_dual(c, emitfn):
    snaps = {k: s.snap() for k, s in c.sems.items()}
    c.dry = True
    emitfn()
    for k, s in c.sems.items():
        s.restore(snaps[k])
    c.dry = False
    emitfn()


def make_consts(nc):
    cs = {}
    cs["ident_f32"] = nc.inline_tensor(np.eye(128, dtype=np.float32), "c_ident")
    cs["ltri"] = nc.inline_tensor(np.triu(np.ones((128, 128), np.float32), 1), "c_ltri")
    a64 = (np.arange(64)[:, None] // 8 == np.arange(64)[None, :] // 8).astype(np.float32)
    cs["a64"] = nc.inline_tensor(a64, "c_a64")
    cs["epart"] = nc.inline_tensor((np.arange(64) // 8).astype(np.float32).reshape(64, 1), "c_epart")
    cm = (np.arange(128)[:, None] <= np.arange(128)[None, :]).astype(np.float32)
    cs["causalT"] = nc.inline_tensor(cm, "c_causalT")
    cs["iota8"] = nc.inline_tensor(np.tile(np.arange(8, dtype=np.float32), (128, 1)), "c_iota8")
    tk = np.zeros((128, 32), np.float32)
    for col in range(32):
        tk[:, col] = (col % 16) * 128 + np.arange(128)
    cs["tokids"] = nc.inline_tensor(tk, "c_tokids")
    trash = np.tile((32 * 128 + np.arange(128, dtype=np.float32))[:, None], (1, 32))
    cs["trash"] = nc.inline_tensor(trash, "c_trash")
    cs["ones_f32_row"] = nc.inline_tensor(np.ones((1, 128), np.float32), "c_onesf")
    cs["ones_bf_row"] = nc.inline_tensor(np.ones((1, 2048), np.float32), "c_onesb")
    e2 = (np.arange(16)[:, None] // 2 == np.arange(8)[None, :]).astype(np.float32)
    cs["e2sel"] = nc.inline_tensor(e2, "c_e2sel")
    lh = (np.arange(16)[:, None] % 2 == (np.arange(128)[None, :] // 64)).astype(np.float32)
    cs["lhsel"] = nc.inline_tensor(lh, "c_lhsel")
    return cs


def build_program(debug=()):
    nc = bass.Bass("TRN2", target_bir_lowering=False)
    c = Ctx()
    c.nc = nc
    c.debug = set(debug)
    c.dbg_bufs = {}
    c.events = {}
    c.dry = False

    io = {}
    io["x_in"] = nc.dram_tensor("x_in", [T, D], F32, kind="ExternalInput").ap()
    for nm in ("wqT", "wkT", "wvT", "woT"):
        io[nm] = nc.dram_tensor(nm, [D, D], F32, kind="ExternalInput").ap()
    for nm in ("bq", "bk", "bv", "bo"):
        io[nm] = nc.dram_tensor(nm, [1, D], F32, kind="ExternalInput").ap()
    io["gamma"] = nc.dram_tensor("gamma", [NH, 1], F32, kind="ExternalInput").ap()
    io["rw"] = nc.dram_tensor("rw", [D, E], F32, kind="ExternalInput").ap()
    io["rb"] = nc.dram_tensor("rb", [1, E], F32, kind="ExternalInput").ap()
    io["w1"] = nc.dram_tensor("w1", [E, D, HF], F32, kind="ExternalInput").ap()
    io["b1"] = nc.dram_tensor("b1", [E, HF], F32, kind="ExternalInput").ap()
    io["w2"] = nc.dram_tensor("w2", [E, HF, D], F32, kind="ExternalInput").ap()
    io["b2"] = nc.dram_tensor("b2", [1, E * D], F32, kind="ExternalInput").ap()
    io["lng"] = nc.dram_tensor("lng", [1, D], F32, kind="ExternalInput").ap()
    io["lnb"] = nc.dram_tensor("lnb", [1, D], F32, kind="ExternalInput").ap()
    io["iotaq"] = nc.dram_tensor("iotaq", [128, T], F32, kind="ExternalInput").ap()
    io["wprev"] = nc.dram_tensor("wprev", [128, 8], F32, kind="ExternalInput").ap()
    io["y_out"] = nc.dram_tensor("y_out", [T, D], F32, kind="ExternalOutput").ap()

    io["x_dram"] = nc.dram_tensor("x_dram", [T, D], F32).ap()
    io["xln_dram"] = nc.dram_tensor("xln_dram", [T, D], BF16).ap()
    io["xin_dram"] = nc.dram_tensor("xin_dram", [2, SLOT, D], BF16).ap()
    io["y_dram"] = nc.dram_tensor("y_dram", [E * SLOT, D], BF16).ap()
    io["idx_dram"] = nc.dram_tensor("idx_dram", [33 * 128, 1], F32).ap()
    io["w1bf"] = nc.dram_tensor("w1bf", [E, D, HF], BF16).ap()
    io["w2bf"] = nc.dram_tensor("w2bf", [E, HF, D], BF16).ap()
    io["attn_dram"] = nc.dram_tensor("attn_dram", [T, D], BF16).ap()
    io["ccvi"] = nc.dram_tensor("ccvi", [128, 32], F32).ap()
    io["ccvo"] = nc.dram_tensor("ccvo", [1024, 32], F32).ap()
    io["ccei"] = nc.dram_tensor("ccei", [128, 32], F32).ap()
    io["cceo"] = nc.dram_tensor("cceo", [1024, 32], F32).ap()
    io["ccli"] = nc.dram_tensor("ccli", [NH * DH, DH], F32).ap()
    io["cclo"] = nc.dram_tensor("cclo", [8 * NH * DH, DH], F32).ap()
    c.io = io
    c.cs = make_consts(nc)

    def dbgf(name, shape, dtype):
        if name in c.debug:
            if name not in c.dbg_bufs:
                c.dbg_bufs[name] = nc.dram_tensor("dbg_" + name, list(shape), dtype,
                                                  kind="ExternalOutput").ap()
            return c.dbg_bufs[name]
        return None

    c.dbgf = dbgf

    with ExitStack() as stack:
        c.stack = stack

        def sb(name, shape, dtype):
            return stack.enter_context(nc.sbuf_tensor(name, shape, dtype))

        c.sbf = sb
        c.sems = {}

        def sem(name):
            if name not in c.sems:
                c.sems[name] = Sem(stack.enter_context(nc.semaphore(name)), name)
            return c.sems[name]

        c.sem = sem

        c.ident_f32 = sb("ident_f32", [128, 128], F32)
        c.ltri = sb("ltri", [128, 128], F32)
        c.a64 = sb("a64", [64, 64], F32)
        c.epart = sb("epart", [64, 1], F32)
        c.causalT = sb("causalT", [128, 128], BF16)
        c.iota8 = sb("iota8", [128, 8], F32)
        c.tokids = sb("tokids", [128, 32], F32)
        c.trash = sb("trash", [128, 32], F32)
        c.ones_f32 = sb("ones_f32", [1, 128], F32)
        c.ones_bf = sb("ones_bf", [1, 2048], BF16)
        c.rwsb = sb("rwsb", [128, DC * E], BF16)
        c.rbsb = sb("rbsb", [1, E], BF16)
        c.b1sb = sb("b1sb", [128, E * NHC], F32)
        c.bqsb = sb("bqsb", [128, DC], F32)
        c.bksb = sb("bksb", [128, DC], F32)
        c.bvsb = sb("bvsb", [1, D], BF16)
        c.bosb = sb("bosb", [1, D], BF16)
        c.lngb = sb("lngb", [128, D], F32)
        c.halflnbb = sb("halflnbb", [128, D], F32)
        c.lngam = sb("lngam", [128, DC], F32)
        c.lngrow16 = sb("lngrow16", [1, 16], F32)
        c.iotaQ = sb("iotaQ", [128, T], F32)
        c.wprev_sb = sb("wprev_sb", [128, 8], F32)
        c.zrow = sb("zrow", [33, 128], F32)

        c.xT = sb("xT", [128, DC * T], BF16)
        c.warena = sb("warena", [128, 2 * 8 * 1024], BF16)
        c.hsb = sb("hsb", [128, NHC * SLOT], BF16)
        c.xinT = sb("xinT", [128, 2 * DC * SLOT], BF16)
        c.uarena = sb("uarena", [128, 8192], F32)
        c.xtile = [sb(f"xtile{j}", [128, D], F32) for j in range(2)]
        c.xlnt = [sb(f"xlnt{j}", [128, D], BF16) for j in range(2)]
        c.scrA = sb("scrA", [128, D], F32)
        c.xg = [c.scrA[:, j * 512:(j + 1) * 512].bitcast(BF16) for j in range(2)]
        c.stat = sb("stat", [128, NT], F32)
        c.stat2 = sb("stat2", [128, NT], F32)
        c.negmu = sb("negmu", [128, NT], F32)
        c.rstd = sb("rstd", [128, NT], F32)
        c.tmpc = sb("tmpc", [128, NT], F32)
        c.tmpc2 = sb("tmpc2", [128, NT], F32)
        c.gbuf = [c.hsb[:, j * 1024:(j + 1) * 1024] for j in range(4)]
        c.ycomb = [c.xinT[:, j * 2048:(j + 1) * 2048].bitcast(F32) for j in range(2)]
        c.ybf = [sb(f"ybf{j}", [128, 512], BF16) for j in range(2)]
        c.outhp = sb("outhp", [128, T], BF16)
        c.asb = [sb(f"asb{j}", [128, 128], BF16) for j in range(2)]
        c.kc = sb("kc", [128, D], BF16)
        c.s0sb = c.uarena[:, 6400:7424]
        c.lsb = c.uarena[0:64, 5376:6400]

        c.Mt = c.uarena[0:64, 0:4096]
        c.eestg = c.uarena[0:64, 4096:6144]
        c.bjunkV = c.uarena[0:64, 4096:8192]
        def bfv(plo, phi, f32lo, f32hi):
            return c.uarena[plo:phi, f32lo:f32hi].bitcast(BF16)

        c.bfv = bfv
        c.stt = c.uarena[:, 2048:2304]
        c.QT0, c.KT0, c.GB0, c.SS0, c.B20, c.BK0 = 2304, 3328, 4352, 0, 6144, 7424
        c.QThp = bfv(0, 128, 2304, 3328)
        c.KThp = bfv(0, 128, 3328, 4352)
        c.gbf = bfv(0, 128, 4352, 5376)
        c.Ssb = bfv(0, 128, 0, 2048)
        c.b2sb = [bfv(0, 1, 6144 + j * 512, 6144 + (j + 1) * 512) for j in range(2)]

        c.logitsT = c.iotaQ[0:8, :]
        c.logits = sb("logits", [128, NT * E], F32)
        c.m1 = sb("m1", [128, NT], F32)
        c.m2 = sb("m2", [128, NT], F32)
        c.eqw = sb("eqw", [128, NT * E], F32)
        c.eqw2 = sb("eqw2", [128, NT * E], F32)
        c.vcand = sb("vcand", [128, 32], F32)
        c.ecand = sb("ecand", [128, 32], F32)
        for nm in ("bis_lo", "bis_hi", "bis_mid", "bis_t1", "bis_t2", "bis_pred",
                   "bis_npred", "pcv", "pcp", "pc"):
            setattr(c, nm, sb(nm, [64, 1], F32))
        c.bistrace = sb("bistrace", [64, 3 * BIS_IT], F32)
        c.throw = sb("throw", [1, 64], F32)
        c.thetas = sb("thetas", [128, E], F32)
        c.flag8 = sb("flag8", [128, E * 32], BF16)
        c.scan8 = sb("scan8", [128, E * 32], BF16)
        c.rowtot = sb("rowtot", [128, E], F32)
        c.rowoff = sb("rowoff", [128, E], F32)
        c.gslot = sb("gslot", [128, 32], F32)
        c.yrow_e = sb("yrow_e", [128, 32], F32)
        c.oke = sb("oke", [128, 32], F32)
        c.flagok = sb("flagok", [128, 32], F32)
        c.yrow = sb("yrow", [128, 32], F32)
        c.sacc = sb("sacc", [128, 32], F32)
        c.wv = sb("wv", [128, 32], F32)
        c.sacc_u8 = sb("sacc_u8", [128, 32], U8)
        c.offs_f = sb("offs_f", [128, 32], F32)
        c.offs_i = sb("offs_i", [128, 32], I32)
        c.idxstg = sb("idxstg", [32, 128], F32)
        c.idxf = sb("idxf", [128, 32], F32)
        c.idxt = sb("idxt", [128, 32], I32)
        c.yrow_i = sb("yrow_i", [128, 32], I32)
        c.tmpcand = sb("tmpcand", [128, 32], F32)

        c.b1stg = c.uarena[0:32, 0:1024]
        c.bqstg = c.uarena[0:8, 1024:1152]
        c.bkstg = c.uarena[0:8, 1152:1280]
        c.gmst = sb("gmst", [16, 1], F32)
        c.e2sb = sb("e2sb", [16, 8], F32)
        c.lngam_rhs = sb("lngam_rhs", [16, 8], F32)
        c.lhselsb = sb("lhselsb", [16, 128], F32)
        c.lngrow = c.uarena[0:1, 1280:2304]
        c.lnbrow = c.uarena[0:1, 2304:3328]

        c.block = stack.enter_context(nc.Block())
        for name in ("xld", "xlnst", "xtld", "wld", "wgld", "wbst", "gq", "ccs",
                     "yst", "outst", "vS", "sS", "pS", "tS"):
            sem(name)

        with nc.psum_tensor("psA", [128, 512], F32) as psA, \
             nc.psum_tensor("psB", [128, 512], F32) as psB, \
             nc.psum_tensor("psC", [128, 512], F32) as psC, \
             nc.psum_tensor("psD", [128, 512], F32) as psD, \
             nc.psum_tensor("psK0", [128, 1024], F32) as psK0, \
             nc.psum_tensor("psK1", [128, 1024], F32) as psK1:
            c.ps = [psA, psB, psC, psD]
            c.psK = [psK0, psK1]

            run_dual(c, lambda: emit_init(c))
            run_dual(c, lambda: emit_ffn(c, 0, io["x_in"], io["x_dram"]))
            if DO_ATTN:
                run_dual(c, lambda: emit_attn(c, io["x_dram"], io["x_dram"]))
            if DO_FFN2:
                run_dual(c, lambda: emit_ffn(c, 1, io["x_dram"], io["y_out"]))
            else:
                last = "a_done" if DO_ATTN else "f0_done"
                run_dual(c, lambda: emit_copy(c, io["x_dram"], io["y_out"], last))
            run_dual(c, lambda: emit_final_fence(c))

    return nc, c


def emit_init(c):
    io, cs = c.io, c.cs
    xld, wgld, vS, sS, tS, pS = (c.sem(n) for n in ("xld", "wgld", "vS", "sS", "tS", "pS"))
    psA, psB = c.ps[0], c.ps[1]

    @sec(c, "sync")
    def _(sy):
        for dst, src in (
            (c.ident_f32[:], cs["ident_f32"].ap()), (c.ltri[:], cs["ltri"].ap()),
            (c.a64[:], cs["a64"].ap()), (c.epart[:], cs["epart"].ap()),
            (c.iota8[:], cs["iota8"].ap()), (c.tokids[:], cs["tokids"].ap()),
            (c.trash[:], cs["trash"].ap()), (c.ones_f32[:], cs["ones_f32_row"].ap()),
            (c.iotaQ[:], io["iotaq"]), (c.wprev_sb[:], io["wprev"]),
            (c.bqstg, io["bq"].rearrange("one (q p) -> (one q) p", q=8)),
            (c.bkstg, io["bk"].rearrange("one (q p) -> (one q) p", q=8)),
            (c.gmst[:], io["gamma"]), (c.e2sb[:], cs["e2sel"].ap()),
            (c.lhselsb[:], cs["lhsel"].ap()),
            (c.lngrow, io["lng"]), (c.lnbrow, io["lnb"]),
        ):
            ins = sy.dma_start(out=dst, in_=src)
            inc(c, ins, xld, 16)
        ins = sy.dma_start(out=c.b1stg.rearrange("q (e p) -> q e p", e=E),
                           in_=io["b1"].rearrange("e (q p) -> q e p", q=32))
        inc(c, ins, xld, 16, "init_ld")

    @sec(c, "gpsimd")
    def _(gp):
        for dst, src in (
            (c.causalT[:], cs["causalT"].ap()), (c.ones_bf[:], cs["ones_bf_row"].ap()),
            (c.rbsb[:], io["rb"]), (c.bvsb[:], io["bv"]), (c.bosb[:], io["bo"]),
        ):
            ins = gp.dma_start(out=dst, in_=src)
            inc(c, ins, wgld, 16)
        ins = gp.dma_start(out=c.rwsb[:].rearrange("p (cc e) -> p cc e", e=E),
                           in_=io["rw"].rearrange("(cc p) e -> p cc e", p=128))
        inc(c, ins, wgld, 16, "init_gld")
        ins = gp.memset(c.zrow[:], 0.0)
        inc(c, ins, pS, 1, "init_zrow")

    @sec(c, "scalar")
    def _(sc):
        wait_ev(c, sc, "scalar", "init_ld")
        ins = sc.activation(out=c.gmst[:], in_=c.gmst[:], func=AF.Ln)
        inc(c, ins, sS, 1, "init_lgam")

    @sec(c, "vector")
    def _(ve):
        wait_ev(c, ve, "vector", "init_lgam")
        ins = ve.tensor_scalar(out=c.lngam_rhs[:], in0=c.e2sb[:], scalar1=c.gmst[:],
                               scalar2=None, op0=ALU.mult)
        inc(c, ins, vS, 1, "init_rhs")

    @sec(c, "tensor")
    def _(te):
        wait_ev(c, te, "tensor", "init_ld")
        wait_ev(c, te, "tensor", "init_rhs")
        for e in range(4):
            te.transpose(psA[:, e * 32:(e + 1) * 32],
                         c.b1stg[:, e * 128:(e + 1) * 128], c.ident_f32[0:32, 0:32])
        for e in range(4, 8):
            te.transpose(psB[:, (e - 4) * 32:(e - 3) * 32],
                         c.b1stg[:, e * 128:(e + 1) * 128], c.ident_f32[0:32, 0:32])
        te.transpose(psA[:, 128:136], c.bqstg, c.ident_f32[0:8, 0:8])
        te.transpose(psA[:, 136:144], c.bkstg, c.ident_f32[0:8, 0:8])
        ins = te.matmul(psA[:, 144:152], c.lhselsb[:], c.lngam_rhs[:], start=True, stop=True)
        inc(c, ins, tS, 1, "init_tr1")

    @sec(c, "vector")
    def _(ve):
        wait_ev(c, ve, "vector", "init_tr1")
        ve.tensor_copy(out=c.b1sb[:, 0:128], in_=psA[:, 0:128])
        ve.tensor_copy(out=c.b1sb[:, 128:256], in_=psB[:, 0:128])
        ve.tensor_copy(out=c.bqsb[:], in_=psA[:, 128:136])
        ve.tensor_copy(out=c.bksb[:], in_=psA[:, 136:144])
        ins = ve.tensor_copy(out=c.lngam[:], in_=psA[:, 144:152])
        inc(c, ins, vS, 1, "init_ev1")

    @sec(c, "tensor")
    def _(te):
        wait_ev(c, te, "tensor", "init_ev1")
        te.matmul(psA[:, 0:512], c.ones_f32[:], c.lngrow[:, 0:512], start=True, stop=True)
        ins = te.matmul(psB[:, 0:512], c.ones_f32[:], c.lngrow[:, 512:1024], start=True, stop=True)
        inc(c, ins, tS, 1, "init_tr2")

    @sec(c, "vector")
    def _(ve):
        wait_ev(c, ve, "vector", "init_tr2")
        ve.tensor_copy(out=c.lngb[:, 0:512], in_=psA[:, 0:512])
        ins = ve.tensor_copy(out=c.lngb[:, 512:1024], in_=psB[:, 0:512])
        inc(c, ins, vS, 1, "init_ev2")

    @sec(c, "tensor")
    def _(te):
        wait_ev(c, te, "tensor", "init_ev2")
        te.matmul(psA[:, 0:512], c.ones_f32[:], c.lnbrow[:, 0:512], start=True, stop=True)
        ins = te.matmul(psB[:, 0:512], c.ones_f32[:], c.lnbrow[:, 512:1024], start=True, stop=True)
        inc(c, ins, tS, 1, "init_tr3")

    @sec(c, "vector")
    def _(ve):
        wait_ev(c, ve, "vector", "init_tr3")
        ve.tensor_scalar(out=c.halflnbb[:, 0:512], in0=psA[:, 0:512], scalar1=0.5,
                         scalar2=None, op0=ALU.mult)
        ins = ve.tensor_scalar(out=c.halflnbb[:, 512:1024], in0=psB[:, 0:512],
                               scalar1=0.5, scalar2=None, op0=ALU.mult)
        inc(c, ins, vS, 1, "init_done")


def emit_ln(c, tag, x_src, wait_before=None):
    xld, xlnst, vS, sS = (c.sem(n) for n in ("xld", "xlnst", "vS", "sS"))

    @sec(c, "sync")
    def _(sy):
        if wait_before is not None:
            wait_ev(c, sy, "sync", wait_before)
        for i in range(NT):
            if i >= 2:
                wait_ev(c, sy, "sync", f"{tag}_norm{i - 2}")
                wait_ev(c, sy, "sync", f"{tag}_sq{i - 2}")
            ins = sy.dma_start(out=c.xtile[i % 2][:], in_=x_src[i * 128:(i + 1) * 128, :])
            inc(c, ins, xld, 16, f"{tag}_x{i}")

    @sec(c, "vector")
    def _(ve):
        if wait_before is not None:
            wait_ev(c, ve, "vector", wait_before)
        for i in range(NT):
            wait_ev(c, ve, "vector", f"{tag}_x{i}")
            xt = c.xtile[i % 2]
            ins = ve.reduce_sum(out=c.stat[:, i:i + 1], in_=xt[:], axis=AXX)
            wait_ev(c, ve, "vector", f"{tag}_sq{i}")
            vf(c, ve, "vector", vS, ins)
            ins = ve.tensor_scalar(out=c.negmu[:, i:i + 1], in0=c.stat[:, i:i + 1], scalar1=-REC,
                                   scalar2=None, op0=ALU.mult)
            ve.tensor_scalar(out=c.tmpc[:, i:i + 1], in0=c.stat2[:, i:i + 1], scalar1=REC,
                             scalar2=None, op0=ALU.mult)
            vf(c, ve, "vector", vS, ins)
            ins = ve.tensor_mul(out=c.tmpc2[:, i:i + 1], in0=c.negmu[:, i:i + 1],
                                in1=c.negmu[:, i:i + 1])
            vf(c, ve, "vector", vS, ins)
            ins = ve.tensor_sub(out=c.tmpc[:, i:i + 1], in0=c.tmpc[:, i:i + 1], in1=c.tmpc2[:, i:i + 1])
            vf(c, ve, "vector", vS, ins)
            ins = ve.tensor_scalar(out=c.tmpc[:, i:i + 1], in0=c.tmpc[:, i:i + 1], scalar1=EPS,
                                   scalar2=None, op0=ALU.add)
            inc(c, ins, vS, 1, f"{tag}_var{i}")
            wait_ev(c, ve, "vector", f"{tag}_rstd{i}")
            ins = ve.tensor_scalar(out=c.xlnt[i % 2][:], in0=xt[:], scalar1=c.negmu[:, i:i + 1],
                                   scalar2=c.rstd[:, i:i + 1], op0=ALU.add, op1=ALU.mult)
            inc(c, ins, vS, 1, f"{tag}_norm{i}")

    @sec(c, "scalar")
    def _(sc):
        if wait_before is not None:
            wait_ev(c, sc, "scalar", wait_before)
        for i in range(NT):
            wait_ev(c, sc, "scalar", f"{tag}_x{i}")
            ins = sc.activation(out=c.scrA[:], in_=c.xtile[i % 2][:], func=AF.Square,
                                accum_out=c.stat2[:, i:i + 1])
            inc(c, ins, sS, 1, f"{tag}_sq{i}")
            wait_ev(c, sc, "scalar", f"{tag}_var{i}")
            sc.activation(out=c.tmpc2[:, i:i + 1], in_=c.tmpc[:, i:i + 1], func=AF.Ln)
            ins = sc.activation(out=c.rstd[:, i:i + 1], in_=c.tmpc2[:, i:i + 1], func=AF.Exp,
                                scale=-0.5)
            inc(c, ins, sS, 1, f"{tag}_rstd{i}")
            wait_ev(c, sc, "scalar", f"{tag}_norm{i}")
            ins = sc.dma_start(out=c.io["xln_dram"][i * 128:(i + 1) * 128, :], in_=c.xlnt[i % 2][:])
            inc(c, ins, xlnst, 16, f"{tag}_xln" if i == NT - 1 else f"{tag}_xlnw{i}")


def emit_ffn(c, ph, x_src, x_dst):
    io = c.io
    tag = f"f{ph}"
    (xld, xlnst, xtld, wld, wgld, wbst, gq, ccs, yst, outst, vS, sS, pS, tS) = (
        c.sem(n) for n in ("xld", "xlnst", "xtld", "wld", "wgld", "wbst", "gq", "ccs",
                           "yst", "outst", "vS", "sS", "pS", "tS"))
    psA, psB, psC, psD = c.ps
    psK0, psK1 = c.psK
    wait_before = "init_done" if ph == 0 else "a_done"
    NWIN = 12

    emit_ln(c, tag, x_src, wait_before=wait_before)

    def win_slot(widx):
        return widx % 2

    def win_srcs(e, wl, bf):
        if wl < 4:
            w1s = io["w1bf"] if bf else io["w1"]
            out = []
            for dp in range(4):
                src = w1s[e, dp * 256:(dp + 1) * 256, wl * 1024:(wl + 1) * 1024] \
                    .rearrange("(two p) h -> p two h", p=128)
                out.append((dp * 2048, 2048, src))
            return out
        j = wl - 4
        nh, q = divmod(j, 4)
        w2s = io["w2bf"] if bf else io["w2"]
        src = w2s[e, q * 1024:(q + 1) * 1024, nh * 512:(nh + 1) * 512] \
            .rearrange("(hq p) n -> p hq n", p=128)
        return [(0, 4096, src)]

    wsem = wgld if ph == 0 else wld
    wq_name = "gpsimd" if ph == 0 else "sync"

    def emit_wload(eng, e, wl):
        widx = e * NWIN + wl
        slot = win_slot(widx)
        if widx >= 2:
            wait_ev(c, eng, wq_name, f"{tag}_windone{widx - 2}")
            if ph == 0:
                wait_ev(c, eng, wq_name, f"{tag}_wb{widx - 2}")
        base = slot * 8 * 1024
        srcs = win_srcs(e, wl, bf=(ph == 1))
        for k, (off, width, src) in enumerate(srcs):
            nblk = src.shape[1]
            dst = c.warena[:, base + off:base + off + width] \
                .rearrange("p (b w) -> p b w", b=nblk)
            ins = eng.dma_start(out=dst, in_=src)
            inc(c, ins, wsem, 16, f"{tag}_wload{widx}" if k == len(srcs) - 1 else None)

    def emit_wb(eng, e, wl):
        widx = e * NWIN + wl
        base = win_slot(widx) * 8 * 1024
        wait_ev(c, eng, "sync", f"{tag}_wload{widx}")
        if wl < 4:
            for dp in range(4):
                dst = io["w1bf"][e, dp * 256:(dp + 1) * 256, wl * 1024:(wl + 1) * 1024] \
                    .rearrange("(two p) h -> p two h", p=128)
                ins = eng.dma_start(
                    out=dst,
                    in_=c.warena[:, base + dp * 2048:base + (dp + 1) * 2048]
                    .rearrange("p (b w) -> p b w", b=2))
                inc(c, ins, wbst, 16, f"{tag}_wb{widx}" if dp == 3 else None)
        else:
            j = wl - 4
            nh, q = divmod(j, 4)
            dst = io["w2bf"][e, q * 1024:(q + 1) * 1024, nh * 512:(nh + 1) * 512] \
                .rearrange("(hq p) n -> p hq n", p=128)
            ins = eng.dma_start(out=dst,
                                in_=c.warena[:, base:base + 4096]
                                .rearrange("p (b w) -> p b w", b=8))
            inc(c, ins, wbst, 16, f"{tag}_wb{widx}")

    def emit_route_gp(gp):
        wait_ev(c, gp, "gpsimd", f"{tag}_cand")
        ins = gp.dma_start(out=io["ccvi"], in_=c.vcand[:])
        inc(c, ins, gq, 16)
        ins = gp.dma_start(out=io["ccei"], in_=c.ecand[:])
        inc(c, ins, gq, 16, f"{tag}_ccin")
        wait_ev(c, gp, "gpsimd", f"{tag}_ccin")
        ins = gp.collective_compute("AllGather", ALU.bypass, replica_groups=RG8,
                                    ins=[io["ccvi"].opt()], outs=[io["ccvo"].opt()])
        cc_inc(c, ins, f"{tag}_ccv")
        ins = gp.collective_compute("AllGather", ALU.bypass, replica_groups=RG8,
                                    ins=[io["ccei"].opt()], outs=[io["cceo"].opt()])
        cc_inc(c, ins, f"{tag}_cce")
        wait_ev(c, gp, "gpsimd", "init_zrow")
        ins = gp.dma_start(out=io["idx_dram"].rearrange("(b p) one -> b (p one)", b=33),
                           in_=c.zrow[:])
        inc(c, ins, gq, 16, f"{tag}_idxclr")
        wait_ev(c, gp, "gpsimd", f"{tag}_offs")
        wait_ev(c, gp, "gpsimd", f"{tag}_idxclr")
        for col in range(32):
            ins = gp.indirect_dma_start(
                out=io["idx_dram"],
                out_offset=IndirectOffsetOnAxis(c.offs_i[:, col:col + 1], 0),
                in_=c.tokids[:, col:col + 1], in_offset=None)
            inc(c, ins, gq, 16, f"{tag}_scat" if col == 31 else None)

    def emit_gather(gp, e):
        wait_ev(c, gp, "gpsimd", f"{tag}_idxt")
        if e >= 2:
            wait_ev(c, gp, "gpsimd", f"{tag}_xinT{e - 2}")
        for rt in range(RT):
            gi = e * RT + rt
            if gi >= 2:
                wait_ev(c, gp, "gpsimd", f"{tag}_xinw{gi - 2}")
            ins = gp.indirect_dma_start(
                out=c.xg[gi % 2],
                out_offset=None,
                in_=io["xln_dram"],
                in_offset=IndirectOffsetOnAxis(c.idxt[:, e * 4 + rt:e * 4 + rt + 1], 0))
            inc(c, ins, gq, 16, f"{tag}_xing{gi}")
            wait_ev(c, gp, "gpsimd", f"{tag}_xing{gi}")
            ins = gp.dma_start(out=io["xin_dram"][e % 2, rt * 128:(rt + 1) * 128, :],
                               in_=c.xg[gi % 2])
            inc(c, ins, gq, 16,
                f"{tag}_gather{e}" if rt == RT - 1 else f"{tag}_xinw{gi}")
        if True:
            alias_ev(c, f"{tag}_xinw{e * RT + RT - 1}", f"{tag}_gather{e}")
        if e >= 2:
            wait_ev(c, gp, "gpsimd", f"{tag}_w2done{e - 2}")
        ins = gp.dma_start(out=c.b2sb[e % 2], in_=io["b2"][:, e * D:(e + 1) * D])
        inc(c, ins, wgld, 16, f"{tag}_b2{e}")

    def emit_comb_gathers(gp):
        for i in range(NT):
            wait_ev(c, gp, "gpsimd", f"{tag}_ydone")
            wait_ev(c, gp, "gpsimd", f"{tag}_yrowi")
            if i >= 2:
                wait_ev(c, gp, "gpsimd", f"{tag}_comb{i - 2}")
            ins = gp.indirect_dma_start(
                out=c.gbuf[(2 * i) % 4], out_offset=None, in_=io["y_dram"],
                in_offset=IndirectOffsetOnAxis(c.yrow_i[:, i:i + 1], 0))
            inc(c, ins, gq, 16)
            ins = gp.indirect_dma_start(
                out=c.gbuf[(2 * i + 1) % 4], out_offset=None, in_=io["y_dram"],
                in_offset=IndirectOffsetOnAxis(c.yrow_i[:, 16 + i:17 + i], 0))
            inc(c, ins, gq, 16, f"{tag}_g{i}")

    if ph == 0:
        @sec(c, "gpsimd")
        def _(gp):
            emit_wload(gp, 0, 0)
            emit_wload(gp, 0, 1)
            emit_route_gp(gp)
            emit_gather(gp, 0)
            for wl in range(2, NWIN):
                emit_wload(gp, 0, wl)
            for e in range(1, E):
                emit_gather(gp, e)
                for wl in range(NWIN):
                    emit_wload(gp, e, wl)
            emit_comb_gathers(gp)
    else:
        @sec(c, "gpsimd")
        def _(gp):
            emit_route_gp(gp)
            for e in range(E):
                emit_gather(gp, e)
            emit_comb_gathers(gp)

    @sec(c, "sync")
    def _(sy):
        if ph == 1:
            emit_wload(sy, 0, 0)
            emit_wload(sy, 0, 1)
        wait_ev(c, sy, "sync", f"{tag}_xln")
        for dcc in range(DC):
            ins = sy.dma_start(out=c.xT[:, dcc * T:(dcc + 1) * T],
                               in_=io["xln_dram"][:, dcc * 128:(dcc + 1) * 128],
                               transpose=True)
            inc(c, ins, xtld, 16, f"{tag}_xT" if dcc == DC - 1 else None)
        wait_ev(c, sy, "sync", f"{tag}_ccv")
        wait_ev(c, sy, "sync", f"{tag}_cce")
        ccv8 = io["ccvo"].rearrange("(cc x) w -> cc (x w)", cc=8)
        cce8 = io["cceo"].rearrange("(cc x) w -> cc (x w)", cc=8)
        for half in range(2):
            if half == 1:
                wait_ev(c, sy, "sync", f"{tag}_bM0")
            for e in range(E):
                ins = sy.dma_start(
                    out=c.Mt[e * 8:(e + 1) * 8, half * 2048:(half + 1) * 2048],
                    in_=ccv8[:, half * 2048:(half + 1) * 2048])
                inc(c, ins, xtld, 16)
                ins = sy.dma_start(
                    out=c.eestg[e * 8:(e + 1) * 8, :],
                    in_=cce8[:, half * 2048:(half + 1) * 2048])
                inc(c, ins, xtld, 16, f"{tag}_mstg{half}" if e == E - 1 else None)
        wait_ev(c, sy, "sync", f"{tag}_scat")
        ins = sy.dma_start(out=c.idxstg[:].rearrange("b (p one) -> b p one", one=1),
                           in_=io["idx_dram"][0:32 * 128, :].rearrange("(b p) one -> b p one", b=32))
        inc(c, ins, xtld, 16, f"{tag}_idxraw")
        for e in range(E):
            wait_ev(c, sy, "sync", f"{tag}_gather{e}")
            if e >= 2:
                wait_ev(c, sy, "sync", f"{tag}_xincons{e - 2}")
            par = e % 2
            for dcc in range(DC):
                ins = sy.dma_start(
                    out=c.xinT[:, par * DC * SLOT + dcc * SLOT:par * DC * SLOT + (dcc + 1) * SLOT],
                    in_=io["xin_dram"][par, :, dcc * 128:(dcc + 1) * 128],
                    transpose=True)
                inc(c, ins, xtld, 16, f"{tag}_xinT{e}" if dcc == DC - 1 else None)
            if ph == 0:
                for wl in range(NWIN):
                    emit_wb(sy, e, wl)
            else:
                for wl in range(NWIN):
                    if e == 0 and wl < 2:
                        continue
                    emit_wload(sy, e, wl)
        for i in range(NT):
            if i >= 2:
                wait_ev(c, sy, "sync", f"{tag}_xo{i - 2}")
            ins = sy.dma_start(out=c.xtile[i % 2][:], in_=x_src[i * 128:(i + 1) * 128, :])
            inc(c, ins, xld, 16, f"{tag}_cx{i}")

    @sec(c, "tensor")
    def _(te):
        wait_ev(c, te, "tensor", f"{tag}_xT")
        wait_ev(c, te, "tensor", "init_gld")
        wait_ev(c, te, "tensor", "init_ld")
        for g in range(4):
            if g >= 1:
                wait_ev(c, te, "tensor", f"{tag}_lgev{g - 1}")
            for dcc in range(DC):
                te.matmul(psA[0:8, 0:512], c.rwsb[:, dcc * E:(dcc + 1) * E],
                          c.xT[:, dcc * T + g * 512: dcc * T + (g + 1) * 512],
                          start=(dcc == 0), stop=False)
            ins = te.matmul(psA[0:8, 0:512], c.rbsb[:], c.ones_bf[0:1, 0:512],
                            start=False, stop=True)
            inc(c, ins, tS, 1, f"{tag}_lgmm{g}")
        for i in range(NT):
            if i < 2:
                wait_ev(c, te, "tensor", f"{tag}_lgev3")
            else:
                wait_ev(c, te, "tensor", f"{tag}_lgtev{i - 2}")
            pb = psB if i % 2 == 0 else psC
            ins = te.transpose(pb[:, 0:8], c.logitsT[:, i * 128:(i + 1) * 128],
                               c.ident_f32[0:8, 0:8])
            inc(c, ins, tS, 1, f"{tag}_lgt{i}")

    @sec(c, "vector")
    def _(ve):
        for g in range(4):
            wait_ev(c, ve, "vector", f"{tag}_lgmm{g}")
            ins = ve.tensor_copy(out=c.logitsT[:, g * 512:(g + 1) * 512], in_=psA[0:8, 0:512])
            inc(c, ins, vS, 1, f"{tag}_lgev{g}")
        for i in range(NT):
            wait_ev(c, ve, "vector", f"{tag}_lgt{i}")
            pb = psB if i % 2 == 0 else psC
            ins = ve.tensor_copy(out=c.logits[:, i * 8:(i + 1) * 8], in_=pb[:, 0:8])
            inc(c, ins, vS, 1, f"{tag}_lgtev{i}")
        lv = c.logits[:].rearrange("p (t e) -> p t e", e=E)
        eqv = c.eqw[:].rearrange("p (t e) -> p t e", e=E)
        eq2v = c.eqw2[:].rearrange("p (t e) -> p t e", e=E)
        i8 = c.iota8[:].unsqueeze(1).broadcast_to([128, NT, E])
        ins = ve.tensor_reduce(out=c.m1[:].unsqueeze(2), in_=lv, axis=AXX, op=ALU.max)
        vf(c, ve, "vector", vS, ins)
        ins = ve.tensor_tensor(out=eqv, in0=lv,
                               in1=c.m1[:].unsqueeze(2).broadcast_to([128, NT, E]), op=ALU.is_equal)
        vf(c, ve, "vector", vS, ins)
        ins = ve.tensor_tensor(out=eq2v, in0=eqv, in1=i8, op=ALU.mult)
        vf(c, ve, "vector", vS, ins)
        ins = ve.tensor_scalar(out=c.eqw[:], in0=c.eqw[:], scalar1=-99.0, scalar2=99.0,
                               op0=ALU.mult, op1=ALU.add)
        vf(c, ve, "vector", vS, ins)
        ins = ve.tensor_add(out=c.eqw2[:], in0=c.eqw2[:], in1=c.eqw[:])
        vf(c, ve, "vector", vS, ins)
        ins = ve.tensor_reduce(out=c.ecand[:, 0:16].unsqueeze(2), in_=eq2v, axis=AXX, op=ALU.min)
        vf(c, ve, "vector", vS, ins)
        ins = ve.tensor_tensor(out=eqv, in0=i8,
                               in1=c.ecand[:, 0:16].unsqueeze(2).broadcast_to([128, NT, E]),
                               op=ALU.is_equal)
        vf(c, ve, "vector", vS, ins)
        ins = ve.scalar_tensor_tensor(out=eq2v, in0=eqv, scalar=-1e30, in1=lv,
                                      op0=ALU.mult, op1=ALU.add)
        vf(c, ve, "vector", vS, ins)
        ins = ve.tensor_reduce(out=c.m2[:].unsqueeze(2), in_=eq2v, axis=AXX, op=ALU.max)
        vf(c, ve, "vector", vS, ins)
        ins = ve.tensor_tensor(out=eqv, in0=eq2v,
                               in1=c.m2[:].unsqueeze(2).broadcast_to([128, NT, E]), op=ALU.is_equal)
        vf(c, ve, "vector", vS, ins)
        ins = ve.tensor_tensor(out=eq2v, in0=eqv, in1=i8, op=ALU.mult)
        vf(c, ve, "vector", vS, ins)
        ins = ve.tensor_scalar(out=c.eqw[:], in0=c.eqw[:], scalar1=-99.0, scalar2=99.0,
                               op0=ALU.mult, op1=ALU.add)
        vf(c, ve, "vector", vS, ins)
        ins = ve.tensor_add(out=c.eqw2[:], in0=c.eqw2[:], in1=c.eqw[:])
        vf(c, ve, "vector", vS, ins)
        ins = ve.tensor_reduce(out=c.ecand[:, 16:32].unsqueeze(2), in_=eq2v, axis=AXX, op=ALU.min)
        vf(c, ve, "vector", vS, ins)
        ins = ve.tensor_sub(out=c.tmpc[:], in0=c.m1[:], in1=c.m2[:])
        inc(c, ins, vS, 1, f"{tag}_m12")
        wait_ev(c, ve, "vector", f"{tag}_sig")
        ins = ve.tensor_scalar(out=c.vcand[:, 16:32], in0=c.vcand[:, 0:16], scalar1=-1.0,
                               scalar2=1.0, op0=ALU.mult, op1=ALU.add)
        inc(c, ins, vS, 1, f"{tag}_cand")

    @sec(c, "scalar")
    def _(sc):
        wait_ev(c, sc, "scalar", f"{tag}_m12")
        ins = sc.activation(out=c.vcand[:, 0:16], in_=c.tmpc[:], func=AF.Sigmoid)
        inc(c, ins, sS, 1, f"{tag}_sig")

    @sec(c, "vector")
    def _(ve):
        ve.memset(c.bis_lo[:], 0.0)
        ve.memset(c.bis_hi[:], 1.0)
        ve.memset(c.bis_mid[:], 0.5)
        for half in range(2):
            wait_ev(c, ve, "vector", f"{tag}_mstg{half}")
            ins = ve.scalar_tensor_tensor(
                out=c.Mt[:, half * 2048:(half + 1) * 2048], in0=c.eestg,
                scalar=c.epart[:], in1=c.Mt[:, half * 2048:(half + 1) * 2048],
                op0=ALU.is_equal, op1=ALU.mult)
            inc(c, ins, vS, 1, f"{tag}_bM{half}")
            wait_abs(c, ve, "vector", vS, vS.n)
        for it in range(BIS_IT):
            ins = ve.tensor_scalar(out=c.bjunkV, in0=c.Mt[:, 0:4096],
                                   scalar1=c.bis_mid[:], scalar2=None,
                                   op0=ALU.is_gt, op1=ALU.add, accum_out=c.pc[:])
            inc(c, ins, vS, 1, f"{tag}_bc{it}")
            wait_ev(c, ve, "vector", f"{tag}_bt{it}")
            if "bist" in c.debug and ph == 0:
                ve.tensor_copy(out=c.bistrace[:, 3 * it:3 * it + 1], in_=c.bis_mid[:])
                ve.tensor_copy(out=c.bistrace[:, 3 * it + 1:3 * it + 2], in_=c.pc[:])
                ve.tensor_copy(out=c.bistrace[:, 3 * it + 2:3 * it + 3], in_=psD[0:64, 0:1])
            ins = ve.tensor_scalar(out=c.bis_pred[:], in0=psD[0:64, 0:1], scalar1=CAP,
                                   scalar2=None, op0=ALU.is_le)
            vf(c, ve, "vector", vS, ins)
            ins = ve.tensor_scalar(out=c.bis_npred[:], in0=c.bis_pred[:], scalar1=-1.0,
                                   scalar2=1.0, op0=ALU.mult, op1=ALU.add)
            ve.tensor_sub(out=c.bis_t1[:], in0=c.bis_mid[:], in1=c.bis_hi[:])
            vf(c, ve, "vector", vS, ins)
            ins = ve.scalar_tensor_tensor(out=c.bis_hi[:], in0=c.bis_t1[:], scalar=c.bis_pred[:],
                                          in1=c.bis_hi[:], op0=ALU.mult, op1=ALU.add)
            ve.tensor_sub(out=c.bis_t2[:], in0=c.bis_mid[:], in1=c.bis_lo[:])
            vf(c, ve, "vector", vS, ins)
            ins = ve.scalar_tensor_tensor(out=c.bis_lo[:], in0=c.bis_t2[:], scalar=c.bis_npred[:],
                                          in1=c.bis_lo[:], op0=ALU.mult, op1=ALU.add)
            vf(c, ve, "vector", vS, ins)
            ins = ve.tensor_add(out=c.bis_t1[:], in0=c.bis_lo[:], in1=c.bis_hi[:])
            vf(c, ve, "vector", vS, ins)
            ins = ve.tensor_scalar(out=c.bis_mid[:], in0=c.bis_t1[:], scalar1=0.5,
                                   scalar2=None, op0=ALU.mult)
            inc(c, ins, vS, 1, f"{tag}_bu{it}")
            wait_abs(c, ve, "vector", vS, vS.n)
        ins = ve.tensor_copy(out=c.bis_t2[:], in_=c.bis_hi[:])
        inc(c, ins, vS, 1, f"{tag}_bisdone")

    @sec(c, "tensor")
    def _(te):
        for it in range(BIS_IT):
            wait_ev(c, te, "tensor", f"{tag}_bc{it}")
            ins = te.matmul(psD[0:64, 0:1], c.a64[:], c.pc[:], start=True, stop=True)
            inc(c, ins, tS, 1, f"{tag}_bt{it}")
        wait_ev(c, te, "tensor", f"{tag}_bisdone")
        ins = te.transpose(psK0[0:1, 0:64], c.bis_hi[:], c.ident_f32[0:64, 0:64])
        inc(c, ins, tS, 1, f"{tag}_throwmm")
        wait_ev(c, te, "tensor", f"{tag}_throwev")
        ins = te.matmul(psA[:, 0:8], c.ones_f32[:], c.throw[:, 0:64:8], start=True, stop=True)
        inc(c, ins, tS, 1, f"{tag}_thbc")
        wait_ev(c, te, "tensor", f"{tag}_rowtot")
        ins = te.matmul(psD[:, 8:16], c.ltri[:], c.rowtot[:], start=True, stop=True)
        inc(c, ins, tS, 1, f"{tag}_rooffmm")
        wait_ev(c, te, "tensor", f"{tag}_idxraw")
        ins = te.transpose(psB[:, 0:32], c.idxstg[:], c.ident_f32[0:32, 0:32])
        inc(c, ins, tS, 1, f"{tag}_idxmm")

    @sec(c, "vector")
    def _(ve):
        wait_ev(c, ve, "vector", f"{tag}_throwmm")
        ins = ve.tensor_copy(out=c.throw[:], in_=psK0[0:1, 0:64])
        inc(c, ins, vS, 1, f"{tag}_throwev")
        wait_ev(c, ve, "vector", f"{tag}_thbc")
        ins = ve.tensor_copy(out=c.thetas[:], in_=psA[:, 0:8])
        inc(c, ins, vS, 1, f"{tag}_thetas")
        wait_abs(c, ve, "vector", vS, vS.n)
        for e in range(E):
            fe = c.flag8[:, e * 32:(e + 1) * 32]
            se = c.scan8[:, e * 32:(e + 1) * 32]
            ins = ve.tensor_scalar(out=fe, in0=c.vcand[:], scalar1=c.thetas[:, e:e + 1],
                                   scalar2=None, op0=ALU.is_gt)
            ve.tensor_scalar(out=c.tmpcand[:], in0=c.ecand[:], scalar1=float(e),
                             scalar2=None, op0=ALU.is_equal)
            vf(c, ve, "vector", vS, ins)
            ins = ve.tensor_mul(out=fe, in0=fe, in1=c.tmpcand[:])
            vf(c, ve, "vector", vS, ins)
            ins = ve.tensor_tensor_scan(out=se, data0=fe, data1=fe, initial=0.0,
                                        op0=ALU.add, op1=ALU.bypass)
            vf(c, ve, "vector", vS, ins)
            ins = ve.tensor_copy(out=c.rowtot[:, e:e + 1], in_=se[:, 31:32])
            if e == E - 1:
                inc(c, ins, vS, 1, f"{tag}_rowtot")
                wait_abs(c, ve, "vector", vS, vS.n)
        wait_ev(c, ve, "vector", f"{tag}_rooffmm")
        ins = ve.tensor_copy(out=c.rowoff[:], in_=psD[:, 8:16])
        inc(c, ins, vS, 1, f"{tag}_rooev")
        wait_abs(c, ve, "vector", vS, vS.n)
        ve.memset(c.yrow[:], 0.0)
        ve.memset(c.sacc[:], 0.0)
        for e in range(E):
            fe = c.flag8[:, e * 32:(e + 1) * 32]
            se = c.scan8[:, e * 32:(e + 1) * 32]
            ins = ve.tensor_sub(out=se, in0=se, in1=fe)
            vf(c, ve, "vector", vS, ins)
            ins = ve.tensor_scalar(out=c.gslot[:], in0=se, scalar1=c.rowoff[:, e:e + 1],
                                   scalar2=None, op0=ALU.add)
            vf(c, ve, "vector", vS, ins)
            ins = ve.tensor_scalar(out=c.oke[:], in0=c.gslot[:], scalar1=float(SLOT),
                                   scalar2=None, op0=ALU.is_lt)
            vf(c, ve, "vector", vS, ins)
            ins = ve.tensor_mul(out=c.flagok[:], in0=fe, in1=c.oke[:])
            ve.tensor_scalar(out=c.yrow_e[:], in0=c.gslot[:], scalar1=float(e * SLOT),
                             scalar2=None, op0=ALU.add)
            vf(c, ve, "vector", vS, ins)
            ins = ve.tensor_mul(out=c.yrow_e[:], in0=c.yrow_e[:], in1=c.flagok[:])
            vf(c, ve, "vector", vS, ins)
            ins = ve.tensor_add(out=c.yrow[:], in0=c.yrow[:], in1=c.yrow_e[:])
            ve.tensor_add(out=c.sacc[:], in0=c.sacc[:], in1=c.flagok[:])
            vf(c, ve, "vector", vS, ins)
        ins = ve.tensor_mul(out=c.wv[:], in0=c.vcand[:], in1=c.sacc[:])
        ve.tensor_copy(out=c.sacc_u8[:], in_=c.sacc[:])
        vf(c, ve, "vector", vS, ins)
        ins = ve.select(out=c.offs_f[:], mask=c.sacc_u8[:], on_true=c.yrow[:], on_false=c.trash[:])
        vf(c, ve, "vector", vS, ins)
        ins = ve.tensor_copy(out=c.offs_i[:], in_=c.offs_f[:])
        inc(c, ins, vS, 1, f"{tag}_offs")
        ins = ve.tensor_copy(out=c.yrow_i[:], in_=c.yrow[:])
        inc(c, ins, vS, 1, f"{tag}_yrowi")
        wait_ev(c, ve, "vector", f"{tag}_idxmm")
        ins = ve.tensor_copy(out=c.idxf[:], in_=psB[:, 0:32])
        vf(c, ve, "vector", vS, ins)
        ins = ve.tensor_copy(out=c.idxt[:], in_=c.idxf[:])
        inc(c, ins, vS, 1, f"{tag}_idxt")

    @sec(c, "tensor")
    def _(te):
        for e in range(E):
            par = e % 2
            xv = c.xinT[:, par * DC * SLOT:(par + 1) * DC * SLOT]
            wait_ev(c, te, "tensor", f"{tag}_xinT{e}")
            if e > 0:
                wait_ev(c, te, "tensor", f"{tag}_yevvdone{e - 1}")
            for q in range(4):
                widx = e * NWIN + q
                wait_ev(c, te, "tensor", f"{tag}_wload{widx}")
                base = win_slot(widx) * 8 * 1024
                for hcl in range(8):
                    hc = q * 8 + hcl
                    ghc = e * NHC + hc
                    pb = psA if hc % 2 == 0 else psB
                    if ghc >= 2:
                        wait_ev(c, te, "tensor", f"{tag}_gelu{ghc - 2}")
                    for dcc in range(DC):
                        dp, two = divmod(dcc, 2)
                        lo = base + dp * 2048 + two * 1024 + hcl * 128
                        ins = te.matmul(pb[:, 0:512], c.warena[:, lo:lo + 128],
                                        xv[:, dcc * SLOT:(dcc + 1) * SLOT],
                                        start=(dcc == 0), stop=(dcc == DC - 1))
                    inc(c, ins, tS, 1, f"{tag}_hmm{ghc}")
                alias_ev(c, f"{tag}_windone{widx}", f"{tag}_hmm{e * NHC + q * 8 + 7}")
            wait_ev(c, te, "tensor", f"{tag}_geludone{e}")
            wait_ev(c, te, "tensor", f"{tag}_b2{e}")
            for nh in range(2):
                for q in range(4):
                    widx = e * NWIN + 4 + nh * 4 + q
                    wait_ev(c, te, "tensor", f"{tag}_wload{widx}")
                    base = win_slot(widx) * 8 * 1024
                    for rt in range(RT):
                        gj = e * 8 + nh * RT + rt
                        pb = c.ps[rt]
                        if q == 0 and gj >= 4:
                            wait_ev(c, te, "tensor", f"{tag}_yevv{gj - 4}")
                        for hcl in range(8):
                            hcg = q * 8 + hcl
                            lhs = c.hsb[:, hcg * SLOT + rt * 128: hcg * SLOT + rt * 128 + 128]
                            ins = te.matmul(pb[:, 0:512], lhs,
                                            c.warena[:, base + hcl * 512:base + (hcl + 1) * 512],
                                            start=(q == 0 and hcl == 0), stop=False,
                                            skip_group_check=True)
                        if rt == RT - 1:
                            inc(c, ins, tS, 1, f"{tag}_w2used{e}_{nh}_{q}")
                            alias_ev(c, f"{tag}_windone{widx}", f"{tag}_w2used{e}_{nh}_{q}")
                for rt in range(RT):
                    gj = e * 8 + nh * RT + rt
                    ins = te.matmul(c.ps[rt][:, 0:512], c.ones_bf[0:1, 0:128],
                                    c.bfv(0, 1, c.B20 + (e % 2) * 512 + nh * 256, c.B20 + (e % 2) * 512 + (nh + 1) * 256),
                                    start=False, stop=True, skip_group_check=True)
                    inc(c, ins, tS, 1, f"{tag}_ymm{gj}")
            alias_ev(c, f"{tag}_w2done{e}", f"{tag}_ymm{e * 8 + 7}")
            alias_ev(c, f"{tag}_xincons{e}", f"{tag}_hmm{e * NHC + 31}")

    @sec(c, "scalar")
    def _(sc):
        for e in range(E):
            for hc in range(NHC):
                ghc = e * NHC + hc
                wait_ev(c, sc, "scalar", f"{tag}_hmm{ghc}")
                if e > 0 and hc == 0:
                    wait_ev(c, sc, "scalar", f"{tag}_w2done{e - 1}")
                pb = psA if hc % 2 == 0 else psB
                ins = sc.activation(out=c.hsb[:, hc * SLOT:(hc + 1) * SLOT], in_=pb[:, 0:512],
                                    func=AF.Gelu, bias=c.b1sb[:, e * NHC + hc:e * NHC + hc + 1])
                inc(c, ins, sS, 1, f"{tag}_gelu{ghc}")
            alias_ev(c, f"{tag}_geludone{e}", f"{tag}_gelu{e * NHC + 31}")
            for j in range(2 * RT):
                gj = e * 8 + j
                nh, rt = divmod(j, RT)
                wait_ev(c, sc, "scalar", f"{tag}_yevv{gj}")
                ins = sc.dma_start(
                    out=io["y_dram"][e * SLOT + rt * 128:e * SLOT + (rt + 1) * 128,
                                     nh * 512:(nh + 1) * 512],
                    in_=c.ybf[gj % 2][:])
                inc(c, ins, yst, 16, f"{tag}_ywr{gj}")
            alias_ev(c, f"{tag}_ydone{e}", f"{tag}_ywr{e * 8 + 7}")
        alias_ev(c, f"{tag}_ydone", f"{tag}_ywr{E * 8 - 1}")

    @sec(c, "vector")
    def _(ve):
        for e in range(E):
            for j in range(2 * RT):
                gj = e * 8 + j
                rt = j % RT
                wait_ev(c, ve, "vector", f"{tag}_ymm{gj}")
                if gj >= 2:
                    wait_ev(c, ve, "vector", f"{tag}_ywr{gj - 2}")
                ins = ve.tensor_copy(out=c.ybf[gj % 2][:], in_=c.ps[rt][:, 0:512])
                inc(c, ins, vS, 1, f"{tag}_yevv{gj}")
            alias_ev(c, f"{tag}_yevvdone{e}", f"{tag}_yevv{e * 8 + 7}")

    @sec(c, "vector")
    def _(ve):
        for i in range(NT):
            wait_ev(c, ve, "vector", f"{tag}_g{i}")
            g0 = c.gbuf[(2 * i) % 4]
            g1 = c.gbuf[(2 * i + 1) % 4]
            yc = c.ycomb[i % 2]
            if i >= 2:
                wait_ev(c, ve, "vector", f"{tag}_csq{i - 2}")
            ins = ve.tensor_scalar(out=yc, in0=g0, scalar1=c.wv[:, i:i + 1],
                                   scalar2=None, op0=ALU.mult)
            vf(c, ve, "vector", vS, ins)
            ins = ve.scalar_tensor_tensor(out=yc, in0=g1, scalar=c.wv[:, 16 + i:17 + i],
                                          in1=yc, op0=ALU.mult, op1=ALU.add,
                                          accum_out=c.stat[:, i:i + 1])
            inc(c, ins, vS, 1, f"{tag}_comb{i}")
            wait_abs(c, ve, "vector", vS, vS.n)
            wait_ev(c, ve, "vector", f"{tag}_csq{i}")
            ins = ve.tensor_scalar(out=c.negmu[:, i:i + 1], in0=c.stat[:, i:i + 1], scalar1=-REC,
                                   scalar2=None, op0=ALU.mult)
            ve.tensor_scalar(out=c.tmpc[:, i:i + 1], in0=c.stat2[:, i:i + 1], scalar1=REC,
                             scalar2=None, op0=ALU.mult)
            vf(c, ve, "vector", vS, ins)
            ins = ve.tensor_mul(out=c.tmpc2[:, i:i + 1], in0=c.negmu[:, i:i + 1],
                                in1=c.negmu[:, i:i + 1])
            vf(c, ve, "vector", vS, ins)
            ins = ve.tensor_sub(out=c.tmpc[:, i:i + 1], in0=c.tmpc[:, i:i + 1],
                                in1=c.tmpc2[:, i:i + 1])
            vf(c, ve, "vector", vS, ins)
            ins = ve.tensor_scalar(out=c.tmpc[:, i:i + 1], in0=c.tmpc[:, i:i + 1], scalar1=EPS,
                                   scalar2=None, op0=ALU.add)
            inc(c, ins, vS, 1, f"{tag}_cvar{i}")
            wait_ev(c, ve, "vector", f"{tag}_crstd{i}")
            ins = ve.tensor_scalar(out=yc, in0=yc, scalar1=c.negmu[:, i:i + 1],
                                   scalar2=c.rstd[:, i:i + 1], op0=ALU.add, op1=ALU.mult)
            vf(c, ve, "vector", vS, ins)
            ins = ve.tensor_mul(out=yc, in0=yc, in1=c.lngb[:])
            vf(c, ve, "vector", vS, ins)
            ins = ve.scalar_tensor_tensor(out=yc, in0=yc, scalar=0.5, in1=c.halflnbb[:],
                                          op0=ALU.mult, op1=ALU.add)
            vf(c, ve, "vector", vS, ins)
            wait_ev(c, ve, "vector", f"{tag}_cx{i}")
            ins = ve.tensor_add(out=c.xtile[i % 2][:], in0=c.xtile[i % 2][:], in1=yc)
            inc(c, ins, vS, 1, f"{tag}_cfin{i}")

    @sec(c, "scalar")
    def _(sc):
        for i in range(NT):
            wait_ev(c, sc, "scalar", f"{tag}_comb{i}")
            ins = sc.activation(out=c.scrA[:], in_=c.ycomb[i % 2], func=AF.Square,
                                accum_out=c.stat2[:, i:i + 1])
            inc(c, ins, sS, 1, f"{tag}_csq{i}")
            wait_ev(c, sc, "scalar", f"{tag}_cvar{i}")
            sc.activation(out=c.tmpc2[:, i:i + 1], in_=c.tmpc[:, i:i + 1], func=AF.Ln)
            ins = sc.activation(out=c.rstd[:, i:i + 1], in_=c.tmpc2[:, i:i + 1], func=AF.Exp,
                                scale=-0.5)
            inc(c, ins, sS, 1, f"{tag}_crstd{i}")
            wait_ev(c, sc, "scalar", f"{tag}_cfin{i}")
            ins = sc.dma_start(out=x_dst[i * 128:(i + 1) * 128, :], in_=c.xtile[i % 2][:])
            inc(c, ins, outst, 16, f"{tag}_xo{i}" if i < NT - 1 else f"{tag}_done")

    if c.debug and ph == 0:
        emit_ffn_debug(c, tag)




def alias_ev(c, new, old):
    if c.dry:
        c.events[new] = c.events[old]


def emit_attn_debug(c):
    io = c.io
    outst = c.sem("outst")

    @sec(c, "sync")
    def _(sy):
        ap = c.dbgf("attnd", [T, D], BF16)
        if ap is not None:
            wait_ev(c, sy, "sync", "a_adw7")
            wait_ev(c, sy, "sync", "a_done")
            for blk in range(0, T, 1024):
                ins = sy.dma_start(out=ap[blk:blk + 1024, :], in_=io["attn_dram"][blk:blk + 1024, :])
                inc(c, ins, outst, 16)
        ap = c.dbgf("ssb", [128, 2048], F32)
        if ap is not None:
            wait_ev(c, sy, "sync", "a_done")
            ins = sy.dma_start(out=ap, in_=c.uarena[:, 0:2048])
            inc(c, ins, outst, 16)
        ap = c.dbgf("s0", [128, 1024], F32)
        if ap is not None:
            wait_ev(c, sy, "sync", "a_done")
            ins = sy.dma_start(out=ap, in_=c.s0sb)
            inc(c, ins, outst, 16)
        ap = c.dbgf("vsbd", [128, NT * D], BF16)
        if ap is not None:
            wait_ev(c, sy, "sync", "a_done")
            ins = sy.dma_start(out=ap, in_=c.hsb[:])
            inc(c, ins, outst, 16)
        ap = c.dbgf("lsbd", [64, 1024], F32)
        if ap is not None:
            wait_ev(c, sy, "sync", "a_done")
            ins = sy.dma_start(out=ap, in_=c.lsb)
            inc(c, ins, outst, 16)
        for nm, src in (("qthp", c.QThp), ("kthp", c.KThp), ("gbfd", c.gbf)):
            ap = c.dbgf(nm, [128, T], BF16)
            if ap is not None:
                wait_ev(c, sy, "sync", "a_done")
                ins = sy.dma_start(out=ap, in_=src)
                inc(c, ins, outst, 16)
        ap = c.dbgf("outhpd", [128, T], BF16)
        if ap is not None:
            wait_ev(c, sy, "sync", "a_done")
            ins = sy.dma_start(out=ap, in_=c.outhp[:])
            inc(c, ins, outst, 16)


def emit_ffn_debug(c, tag):
    io = c.io
    outst = c.sem("outst")
    taps_sb = [
        ("logits", c.logits, [128, NT * E], F32, f"{tag}_cand"),
        ("vcand", c.vcand, [128, 32], F32, f"{tag}_cand"),
        ("ecand", c.ecand, [128, 32], F32, f"{tag}_cand"),
        ("theta", c.thetas, [128, E], F32, f"{tag}_thetas"),
        ("yrow", c.yrow, [128, 32], F32, f"{tag}_yrowi"),
        ("sacc", c.sacc, [128, 32], F32, f"{tag}_yrowi"),
        ("wv", c.wv, [128, 32], F32, f"{tag}_yrowi"),
        ("idxf", c.idxf, [128, 32], F32, f"{tag}_idxt"),
        ("rowtot", c.rowtot, [128, E], F32, f"{tag}_rooev"),
        ("rowoff", c.rowoff, [128, E], F32, f"{tag}_rooev"),
        ("hi", c.bis_hi, [64, 1], F32, f"{tag}_bisdone"),
        ("bist", c.bistrace, [64, 3 * BIS_IT], F32, f"{tag}_bisdone"),
    ]

    @sec(c, "sync")
    def _(sy):
        for name, src, shape, dt_, ev in taps_sb:
            ap = c.dbgf(name, shape, dt_)
            if ap is None:
                continue
            wait_ev(c, sy, "sync", ev)
            wait_ev(c, sy, "sync", f"{tag}_done")
            try:
                srcap = src[:]
            except Exception:
                srcap = src
            ins = sy.dma_start(out=ap, in_=srcap)
            inc(c, ins, outst, 16)
        for name, src, shape, ev in (
            ("xln", io["xln_dram"], [T, D], f"{tag}_xln"),
            ("ydram", io["y_dram"], [E * SLOT, D], f"{tag}_ydone"),
            ("idxdram", io["idx_dram"], [33 * 128, 1], f"{tag}_scat"),
        ):
            ap = c.dbgf(name, shape, BF16 if name in ("xln", "ydram") else F32)
            if ap is None:
                continue
            wait_ev(c, sy, "sync", ev)
            wait_ev(c, sy, "sync", f"{tag}_done")
            for blk in range(0, shape[0], 1024):
                hi_ = min(blk + 1024, shape[0])
                ins = sy.dma_start(out=ap[blk:hi_, :], in_=src[blk:hi_, :])
                inc(c, ins, outst, 16)


def emit_attn(c, x_src, x_dst):
    io = c.io
    tag = "a"
    (xld, xlnst, xtld, wld, wgld, gq, yst, outst, vS, sS, pS, tS) = (
        c.sem(n) for n in ("xld", "xlnst", "xtld", "wld", "wgld", "gq",
                           "yst", "outst", "vS", "sS", "pS", "tS"))
    psA, psB, psC, psD = c.ps
    psK0, psK1 = c.psK
    Vsb = c.hsb
    lstg = c.ycomb
    WV, WK, WQ, WO = 0, 1, 0, 1
    c.bkrow = c.bfv(0, 1, 7424, 7936)

    emit_ln(c, tag, x_src, wait_before="f0_done")

    @sec(c, "gpsimd")
    def _(gp):
        for nm, slot in (("wvT", WV), ("wkT", WK)):
            for dcc in range(DC):
                ins = gp.dma_start(
                    out=c.warena[:, slot * 8192 + dcc * 1024:slot * 8192 + (dcc + 1) * 1024],
                    in_=io[nm][dcc * 128:(dcc + 1) * 128, :])
                inc(c, ins, wgld, 16, f"a_{nm}" if dcc == DC - 1 else None)
        ins = gp.dma_start(out=c.bkrow, in_=io["bk"])
        inc(c, ins, wgld, 16, "a_bkrow")
        wait_ev(c, gp, "gpsimd", "a_lev")
        ins = gp.dma_start(out=io["ccli"].rearrange("(h dk) dv -> dk h dv", h=NH),
                           in_=c.lsb.rearrange("dk (h dv) -> dk h dv", h=NH))
        inc(c, ins, gq, 16, "a_lwr")
        wait_ev(c, gp, "gpsimd", "a_lwr")
        ins = gp.collective_compute("AllGather", ALU.bypass, replica_groups=RG8,
                                    ins=[io["ccli"].opt()], outs=[io["cclo"].opt()])
        cc_inc(c, ins, "a_ccl")
        wait_ev(c, gp, "gpsimd", "a_ccl")
        for j in range(8):
            if j >= 2:
                wait_ev(c, gp, "gpsimd", f"a_s0acc{j - 2}")
            src = io["cclo"][j * 1024:(j + 1) * 1024, :] \
                .rearrange("(h dk) dv -> dk h dv", h=NH)
            ins = gp.dma_start(out=lstg[j % 2][0:64, :].rearrange("p (h dv) -> p h dv", h=NH),
                               in_=src)
            inc(c, ins, gq, 16)
            ins = gp.dma_start(out=lstg[j % 2][64:128, :].rearrange("p (h dv) -> p h dv", h=NH),
                               in_=src)
            inc(c, ins, gq, 16, f"a_lstg{j}")
        wait_ev(c, gp, "gpsimd", "a_vprojdone")
        for dcc in range(DC):
            ins = gp.dma_start(
                out=c.warena[:, WQ * 8192 + dcc * 1024:WQ * 8192 + (dcc + 1) * 1024],
                in_=io["wqT"][dcc * 128:(dcc + 1) * 128, :])
            inc(c, ins, wgld, 16, "a_wqT" if dcc == DC - 1 else None)
        wait_ev(c, gp, "gpsimd", "a_ktlast")
        for dcc in range(DC):
            ins = gp.dma_start(
                out=c.warena[:, WO * 8192 + dcc * 1024:WO * 8192 + (dcc + 1) * 1024],
                in_=io["woT"][dcc * 128:(dcc + 1) * 128, :])
            inc(c, ins, wgld, 16, "a_woT" if dcc == DC - 1 else None)

    @sec(c, "sync")
    def _(sy):
        wait_ev(c, sy, "sync", "a_xln")
        for dcc in range(DC):
            ins = sy.dma_start(out=c.xT[:, dcc * T:(dcc + 1) * T],
                               in_=io["xln_dram"][:, dcc * 128:(dcc + 1) * 128],
                               transpose=True)
            inc(c, ins, xtld, 16, "a_xT" if dcc == DC - 1 else None)
        ins = sy.dma_start(out=c.iotaQ[:], in_=io["iotaq"])
        inc(c, ins, xtld, 16, "a_iotaq")
        for gp2 in range(8):
            wait_ev(c, sy, "sync", f"a_outhp{gp2}")
            ins = sy.dma_start(
                out=io["attn_dram"][:, gp2 * 128:(gp2 + 1) * 128]
                .rearrange("(ck p) d -> p ck d", p=128),
                in_=c.outhp[:].rearrange("p (ck d) -> p ck d", d=128))
            inc(c, ins, yst, 16, f"a_adw{gp2}")
        for ch in range(DC):
            wait_ev(c, sy, "sync", f"a_adw{ch}")
            if ch == 0:
                wait_ev(c, sy, "sync", "a_qtlast")
                wait_ev(c, sy, "sync", "a_ktlast")
            ins = sy.dma_start(out=c.xT[:, ch * T:(ch + 1) * T],
                               in_=io["attn_dram"][:, ch * 128:(ch + 1) * 128],
                               transpose=True)
            inc(c, ins, xtld, 16, f"a_ard{ch}")
        for i in range(NT):
            if i >= 2:
                wait_ev(c, sy, "sync", f"a_xo{i - 2}")
            ins = sy.dma_start(out=c.xtile[i % 2][:], in_=x_src[i * 128:(i + 1) * 128, :])
            inc(c, ins, xld, 16, f"a_cx{i}")

    @sec(c, "tensor")
    def _(te):
        wait_ev(c, te, "tensor", "a_xT")
        wait_ev(c, te, "tensor", "a_wvT")
        wait_ev(c, te, "tensor", "f0_yevvdone7")
        wait_ev(c, te, "tensor", "f0_geludone7")
        wait_ev(c, te, "tensor", "f0_throwev")
        for vt in range(NT):
            for nh in range(2):
                j = vt * 2 + nh
                pb = psC if j % 2 == 0 else psD
                if j >= 2:
                    wait_ev(c, te, "tensor", f"a_vev{j - 2}")
                for dcc in range(DC):
                    te.matmul(pb[:, 0:512],
                              c.xT[:, dcc * T + vt * 128:dcc * T + (vt + 1) * 128],
                              c.warena[:, WV * 8192 + dcc * 1024 + nh * 512:
                                        WV * 8192 + dcc * 1024 + (nh + 1) * 512],
                              start=(dcc == 0), stop=False)
                ins = te.matmul(pb[:, 0:512], c.ones_bf[0:1, 0:128],
                                c.bvsb[0:1, nh * 512:(nh + 1) * 512], start=False, stop=True)
                inc(c, ins, tS, 1, f"a_vmm{j}")
        alias_ev(c, "a_vprojdone", f"a_vmm{2 * NT - 1}")
        wait_ev(c, te, "tensor", "a_wkT")
        wait_ev(c, te, "tensor", "a_bkrow")
        first = True
        for ck in range(NCHUNK):
            for nh in range(2):
                pb = psA if nh == 0 else psB
                if ck >= 1:
                    wait_ev(c, te, "tensor", f"a_kcev{ck - 1}_{nh}")
                for dcc in range(DC):
                    te.matmul(pb[:, 0:512],
                              c.xT[:, dcc * T + ck * 128:dcc * T + (ck + 1) * 128],
                              c.warena[:, WK * 8192 + dcc * 1024 + nh * 512:
                                        WK * 8192 + dcc * 1024 + (nh + 1) * 512],
                              start=(dcc == 0), stop=False)
                ins = te.matmul(pb[:, 0:512], c.ones_bf[0:1, 0:128],
                                c.bfv(0, 1, c.BK0 + nh * 256, c.BK0 + (nh + 1) * 256), start=False, stop=True)
                inc(c, ins, tS, 1, f"a_kcmm{ck}_{nh}")
            wait_ev(c, te, "tensor", f"a_kcev{ck}_0")
            wait_ev(c, te, "tensor", f"a_kcev{ck}_1")
            for h in range(NH):
                ins = te.matmul(psK0[0:64, h * 64:(h + 1) * 64],
                                c.kc[:, h * 64:(h + 1) * 64],
                                Vsb[:, ck * 1024 + h * 64:ck * 1024 + (h + 1) * 64],
                                start=first, stop=False, skip_group_check=True)
                first = False
            if ck == NCHUNK - 1:
                inc(c, ins, tS, 1, "a_lmm")
        for half in range(2):
            wait_ev(c, te, "tensor", f"a_sthalf{half}")
            for ck in range(NCHUNK):
                pb = psA if ck % 2 == 0 else psB
                if ck >= 2:
                    wait_ev(c, te, "tensor", f"a_kc2ev{half}_{ck - 2}")
                for dcc in range(DC):
                    te.matmul(pb[:, 0:512],
                              c.xT[:, dcc * T + ck * 128:dcc * T + (ck + 1) * 128],
                              c.warena[:, WK * 8192 + dcc * 1024 + half * 512:
                                        WK * 8192 + dcc * 1024 + (half + 1) * 512],
                              start=(dcc == 0), stop=False)
                ins = te.matmul(pb[:, 0:512], c.ones_bf[0:1, 0:128],
                                c.bfv(0, 1, c.BK0 + half * 256, c.BK0 + (half + 1) * 256), start=False, stop=True)
                inc(c, ins, tS, 1, f"a_kc2mm{half}_{ck}")
                pk = c.psK[ck % 2]
                wait_ev(c, te, "tensor", f"a_kc2ev{half}_{ck}")
                if ck >= 2:
                    wait_ev(c, te, "tensor", f"a_stupd{half}_{ck - 2}")
                for hh in range(8):
                    hl, pr = divmod(hh, 2)
                    gh = half * 8 + hh
                    ins = te.matmul(pk[pr * 64:(pr + 1) * 64, hl * 64:(hl + 1) * 64],
                                    c.kc[:, hh * 64:(hh + 1) * 64],
                                    Vsb[:, ck * 1024 + gh * 64:ck * 1024 + (gh + 1) * 64],
                                    start=(hh == 0 or hh == 1), stop=False,
                                    skip_group_check=True,
                                    tile_position=(0, 64) if pr == 1 else None)
                inc(c, ins, tS, 1, f"a_kv{half}_{ck}")
            for hpl in range(4):
                gp2 = half * 4 + hpl
                wait_ev(c, te, "tensor", "a_wqT")
                if gp2 > 0:
                    wait_ev(c, te, "tensor", f"a_outhp{gp2 - 1}")
                for g in range(4):
                    pb = psC if g % 2 == 0 else psD
                    if g >= 2:
                        wait_ev(c, te, "tensor", f"a_qtev{gp2}_{g - 2}")
                    for dcc in range(DC):
                        ins = te.matmul(pb[:, 0:512],
                                        c.warena[:, WQ * 8192 + dcc * 1024 + gp2 * 128:
                                                  WQ * 8192 + dcc * 1024 + (gp2 + 1) * 128],
                                        c.xT[:, dcc * T + g * 512:dcc * T + (g + 1) * 512],
                                        start=(dcc == 0), stop=(dcc == DC - 1))
                    inc(c, ins, tS, 1, f"a_qtmm{gp2}_{g}")
                if gp2 == 7:
                    alias_ev(c, "a_qtlast", "a_qtmm7_3")
                for g in range(4):
                    pb = psC if g % 2 == 0 else psD
                    wait_ev(c, te, "tensor",
                            f"a_qtev{gp2}_{g + 2}" if g < 2 else f"a_ktev{gp2}_{g - 2}")
                    for dcc in range(DC):
                        ins = te.matmul(pb[:, 0:512],
                                        c.warena[:, WK * 8192 + dcc * 1024 + gp2 * 128:
                                                  WK * 8192 + dcc * 1024 + (gp2 + 1) * 128],
                                        c.xT[:, dcc * T + g * 512:dcc * T + (g + 1) * 512],
                                        start=(dcc == 0), stop=(dcc == DC - 1))
                    inc(c, ins, tS, 1, f"a_ktmm{gp2}_{g}")
                if gp2 == 7:
                    alias_ev(c, "a_ktlast", "a_ktmm7_3")
                if hpl == 0:
                    wait_ev(c, te, "tensor", f"a_kc2ev{half}_{NCHUNK - 1}")
                for cg in range(4):
                    for ci in range(4):
                        ck = cg * 4 + ci
                        g = ck // 4
                        wait_ev(c, te, "tensor", f"a_qtev{gp2}_{g}")
                        wait_ev(c, te, "tensor", f"a_ktev{gp2}_{g}")
                        for pr in range(2):
                            ja = ck * 2 + pr
                            pa = psA if ja % 2 == 0 else psB
                            if ja >= 2:
                                wait_ev(c, te, "tensor", f"a_amask{gp2}_{ja - 2}")
                            ins = te.matmul(pa[:, 0:128],
                                            c.bfv(pr * 64, (pr + 1) * 64, c.KT0 + ck * 64, c.KT0 + (ck + 1) * 64),
                                            c.bfv(pr * 64, (pr + 1) * 64, c.QT0 + ck * 64, c.QT0 + (ck + 1) * 64),
                                            start=True, stop=True)
                            inc(c, ins, tS, 1, f"a_amm{gp2}_{ja}")
                        po = psC if cg % 2 == 0 else psD
                        if ci == 0 and cg >= 2:
                            wait_ev(c, te, "tensor", f"a_outev{gp2}_{cg - 2}")
                        for pr in range(2):
                            ja = ck * 2 + pr
                            gh = gp2 * 2 + pr
                            wait_ev(c, te, "tensor", f"a_amask{gp2}_{ja}")
                            te.matmul(po[:, ci * 128 + pr * 64:ci * 128 + (pr + 1) * 64],
                                      c.asb[ja % 2][:],
                                      Vsb[:, ck * 1024 + gh * 64:ck * 1024 + (gh + 1) * 64],
                                      start=True, stop=False, skip_group_check=True)
                            ins = te.matmul(po[:, ci * 128 + pr * 64:ci * 128 + (pr + 1) * 64],
                                            c.bfv(pr * 64, (pr + 1) * 64, c.QT0 + ck * 64, c.QT0 + (ck + 1) * 64),
                                            c.bfv(pr * 64, (pr + 1) * 64,
                                                  ck * 128 + hpl * 32, ck * 128 + (hpl + 1) * 32),
                                            start=False, stop=True, skip_group_check=True)
                        inc(c, ins, tS, 1, f"a_omm{gp2}_{cg}" if ci == 3 else None)
        wait_ev(c, te, "tensor", "a_woT")
        for i in range(NT):
            for nh in range(2):
                j = i * 2 + nh
                pb = psA if j % 2 == 0 else psB
                if j >= 2:
                    wait_ev(c, te, "tensor", f"a_xaddh{j - 2}")
                for ch in range(DC):
                    wait_ev(c, te, "tensor", f"a_ard{ch}")
                    te.matmul(pb[:, 0:512],
                              c.xT[:, ch * T + i * 128:ch * T + (i + 1) * 128],
                              c.warena[:, WO * 8192 + ch * 1024 + nh * 512:
                                        WO * 8192 + ch * 1024 + (nh + 1) * 512],
                              start=(ch == 0), stop=False)
                ins = te.matmul(pb[:, 0:512], c.ones_bf[0:1, 0:128],
                                c.bosb[0:1, nh * 512:(nh + 1) * 512], start=False, stop=True)
                inc(c, ins, tS, 1, f"a_womm{j}")

    @sec(c, "vector")
    def _(ve):
        wait_ev(c, ve, "vector", "a_lmm")
        ins = ve.tensor_copy(out=c.lsb, in_=psK0[0:64, 0:1024])
        inc(c, ins, vS, 1, "a_lev")
        for j in range(8):
            wait_ev(c, ve, "vector", f"a_lstg{j}")
            if j == 0:
                ins = ve.tensor_scalar(out=c.s0sb, in0=lstg[0], scalar1=c.wprev_sb[:, 0:1],
                                       scalar2=None, op0=ALU.mult)
            else:
                ins = ve.scalar_tensor_tensor(out=c.s0sb, in0=lstg[j % 2],
                                              scalar=c.wprev_sb[:, j:j + 1], in1=c.s0sb,
                                              op0=ALU.mult, op1=ALU.add)
            inc(c, ins, vS, 1, f"a_s0acc{j}")
            wait_abs(c, ve, "vector", vS, vS.n)
        for half in range(2):
            sv = c.s0sb[:, half * 512:(half + 1) * 512]
            for pr in range(2):
                ins = ve.tensor_copy(
                    out=c.stt[pr * 64:(pr + 1) * 64, :],
                    in_=sv[pr * 64:(pr + 1) * 64, :]
                    .rearrange("p (hl pr2 dv) -> p hl pr2 dv", hl=4, pr2=2)[:, :, pr, :])
                inc(c, ins, vS, 1, f"a_sthalf{half}" if pr == 1 else None)
            wait_abs(c, ve, "vector", vS, vS.n)
            for ck in range(NCHUNK):
                ins = ve.tensor_copy(out=c.bfv(0, 128, ck * 128, (ck + 1) * 128), in_=c.stt)
                vf(c, ve, "vector", vS, ins)
                wait_ev(c, ve, "vector", f"a_kv{half}_{ck}")
                ins = ve.tensor_add(out=c.stt, in0=c.stt, in1=c.psK[ck % 2][:, 0:256])
                inc(c, ins, vS, 1, f"a_stupd{half}_{ck}")
                wait_abs(c, ve, "vector", vS, vS.n)
            for hpl in range(4):
                gp2 = half * 4 + hpl
                for h2 in range(2):
                    wait_ev(c, ve, "vector", "a_iotaq")
                    if gp2 > 0 or h2 > 0:
                        pgp2, ph2 = (gp2, 0) if h2 == 1 else (gp2 - 1, 1)
                        wait_ev(c, ve, "vector", f"a_gbfh{pgp2}_{ph2}")
                    ins = ve.tensor_scalar(out=c.scrA[:], in0=c.iotaQ[:, h2 * 1024:(h2 + 1) * 1024],
                                           scalar1=c.lngam[:, gp2:gp2 + 1], scalar2=None,
                                           op0=ALU.mult)
                    inc(c, ins, vS, 1, f"a_gps{gp2}_{h2}")
                for g in range(4):
                    wait_ev(c, ve, "vector", f"a_qtmm{gp2}_{g}")
                    wait_ev(c, ve, "vector", f"a_gbf{gp2}")
                    pb = psC if g % 2 == 0 else psD
                    ins = ve.scalar_tensor_tensor(
                        out=c.bfv(0, 128, c.QT0 + g * 256, c.QT0 + (g + 1) * 256), in0=pb[:, 0:512],
                        scalar=c.bqsb[:, gp2:gp2 + 1], in1=c.bfv(0, 128, c.GB0 + g * 256, c.GB0 + (g + 1) * 256),
                        op0=ALU.add, op1=ALU.mult)
                    inc(c, ins, vS, 1, f"a_qtev{gp2}_{g}")
                for cg in range(4):
                    for ci in range(4):
                        ck = cg * 4 + ci
                        for pr in range(2):
                            ja = ck * 2 + pr
                            wait_ev(c, ve, "vector", f"a_amm{gp2}_{ja}")
                            pa = psA if ja % 2 == 0 else psB
                            ins = ve.tensor_tensor(out=c.asb[ja % 2][:], in0=pa[:, 0:128],
                                                   in1=c.causalT[:], op=ALU.mult)
                            inc(c, ins, vS, 1, f"a_amask{gp2}_{ja}")
                    wait_ev(c, ve, "vector", f"a_omm{gp2}_{cg}")
                    if gp2 > 0 and cg == 0:
                        wait_ev(c, ve, "vector", f"a_adw{gp2 - 1}")
                    po = psC if cg % 2 == 0 else psD
                    ins = ve.tensor_copy(out=c.outhp[:, cg * 512:(cg + 1) * 512], in_=po[:, 0:512])
                    inc(c, ins, vS, 1, f"a_outev{gp2}_{cg}")
                alias_ev(c, f"a_outhp{gp2}", f"a_outev{gp2}_3")
        for i in range(NT):
            for nh in range(2):
                j = i * 2 + nh
                wait_ev(c, ve, "vector", f"a_womm{j}")
                wait_ev(c, ve, "vector", f"a_cx{i}")
                pb = psA if j % 2 == 0 else psB
                ins = ve.tensor_add(out=c.xtile[i % 2][:, nh * 512:(nh + 1) * 512],
                                    in0=c.xtile[i % 2][:, nh * 512:(nh + 1) * 512],
                                    in1=pb[:, 0:512])
                inc(c, ins, vS, 1, f"a_xaddh{j}")

    @sec(c, "scalar")
    def _(sc):
        for vt in range(NT):
            for nh in range(2):
                j = vt * 2 + nh
                wait_ev(c, sc, "scalar", f"a_vmm{j}")
                pb = psC if j % 2 == 0 else psD
                ins = sc.copy(out=Vsb[:, vt * 1024 + nh * 512:vt * 1024 + (nh + 1) * 512],
                              in_=pb[:, 0:512])
                inc(c, ins, sS, 1, f"a_vev{j}")
        for ck in range(NCHUNK):
            for nh in range(2):
                wait_ev(c, sc, "scalar", f"a_kcmm{ck}_{nh}")
                pb = psA if nh == 0 else psB
                ins = sc.copy(out=c.kc[:, nh * 512:(nh + 1) * 512], in_=pb[:, 0:512])
                inc(c, ins, sS, 1, f"a_kcev{ck}_{nh}")
        for half in range(2):
            for ck in range(NCHUNK):
                wait_ev(c, sc, "scalar", f"a_kc2mm{half}_{ck}")
                pb = psA if ck % 2 == 0 else psB
                ins = sc.copy(out=c.kc[:, 0:512], in_=pb[:, 0:512])
                inc(c, ins, sS, 1, f"a_kc2ev{half}_{ck}")
            for hpl in range(4):
                gp2 = half * 4 + hpl
                for h2 in range(2):
                    wait_ev(c, sc, "scalar", f"a_gps{gp2}_{h2}")
                    if gp2 > 0 and h2 == 0:
                        wait_ev(c, sc, "scalar", f"a_qtev{gp2 - 1}_3")
                    ins = sc.activation(out=c.bfv(0, 128, c.GB0 + h2 * 512, c.GB0 + (h2 + 1) * 512), in_=c.scrA[:],
                                        func=AF.Exp)
                    inc(c, ins, sS, 1, f"a_gbfh{gp2}_{h2}")
                    if h2 == 1:
                        alias_ev(c, f"a_gbf{gp2}", f"a_gbfh{gp2}_{h2}")
                for g in range(4):
                    wait_ev(c, sc, "scalar", f"a_ktmm{gp2}_{g}")
                    pb = psC if g % 2 == 0 else psD
                    ins = sc.activation(out=c.bfv(0, 128, c.KT0 + g * 256, c.KT0 + (g + 1) * 256), in_=pb[:, 0:512],
                                        func=AF.Identity, bias=c.bksb[:, gp2:gp2 + 1])
                    inc(c, ins, sS, 1, f"a_ktev{gp2}_{g}")
        for i in range(NT):
            wait_ev(c, sc, "scalar", f"a_xaddh{i * 2}")
            wait_ev(c, sc, "scalar", f"a_xaddh{i * 2 + 1}")
            ins = sc.dma_start(out=x_dst[i * 128:(i + 1) * 128, :], in_=c.xtile[i % 2][:])
            inc(c, ins, outst, 16, f"a_xo{i}" if i < NT - 1 else "a_done")

    if c.debug:
        emit_attn_debug(c)


def emit_copy(c, src, dst, wait_before):
    outst = c.sem("outst")

    @sec(c, "sync")
    def _(sy):
        wait_ev(c, sy, "sync", wait_before)
        for i in range(NT):
            ins = sy.dma_start(out=dst[i * 128:(i + 1) * 128, :], in_=src[i * 128:(i + 1) * 128, :])
            inc(c, ins, outst, 16, "copy_done" if i == NT - 1 else None)


def emit_final_fence(c):
    outst = c.sem("outst")

    @sec(c, "sync")
    def _(sy):
        wait_abs(c, sy, "sync", outst, outst.n)


DEBUG_TAPS = ()
LAST_RESULTS = None


def kernel(**inputs):
    global LAST_RESULTS
    n = 8
    x = np.asarray(inputs["L0_attn_x"], dtype=np.float32).reshape(2 * 8192, D)
    nc, c = build_program(debug=DEBUG_TAPS)
    f32 = lambda a: np.ascontiguousarray(np.asarray(a), dtype=np.float32)
    wqT = f32(np.asarray(inputs["L0_attn_Wq"]).T)
    wkT = f32(np.asarray(inputs["L0_attn_Wk"]).T)
    wvT = f32(np.asarray(inputs["L0_attn_Wv"]).T)
    woT = f32(np.asarray(inputs["L0_attn_Wo"]).T)
    base = {
        "wqT": wqT, "wkT": wkT, "wvT": wvT, "woT": woT,
        "bq": f32(inputs["L0_attn_bq"]).reshape(1, D),
        "bk": f32(inputs["L0_attn_bk"]).reshape(1, D),
        "bv": f32(inputs["L0_attn_bv"]).reshape(1, D),
        "bo": f32(inputs["L0_attn_bo"]).reshape(1, D),
        "gamma": f32(inputs["L0_attn_gamma"]).reshape(NH, 1),
        "rw": f32(inputs["L0_ffn_router_w"]),
        "rb": f32(inputs["L0_ffn_router_b"]).reshape(1, E),
        "w1": f32(inputs["L0_ffn_w1"]),
        "b1": f32(inputs["L0_ffn_b1"]),
        "w2": f32(inputs["L0_ffn_w2"]),
        "b2": f32(inputs["L0_ffn_b2"]).reshape(1, E * D),
        "lng": f32(inputs["L0_ffn_ln_g"]).reshape(1, D),
        "lnb": f32(inputs["L0_ffn_ln_b"]).reshape(1, D),
    }
    in_maps = []
    for cc_ in range(n):
        seq_off = (cc_ % 4) * T
        iq = np.tile(np.arange(seq_off, seq_off + T, dtype=np.float32), (128, 1))
        wp = np.zeros((128, 8), np.float32)
        grp = range(4 * (cc_ // 4), cc_)
        for j in grp:
            wp[:, j] = 1.0
        m = dict(base)
        m["x_in"] = np.ascontiguousarray(x[cc_ * T:(cc_ + 1) * T])
        m["iotaq"] = iq
        m["wprev"] = wp
        in_maps.append(m)
    res = run_bass_kernel_spmd(nc, in_maps, core_ids=list(range(n)))
    LAST_RESULTS = res.results
    out = np.concatenate([res.results[cc_]["y_out"] for cc_ in range(n)], axis=0)
    return out.reshape(2, 8192, D)


# version 03
# speedup vs baseline: 1.0254x; 1.0254x over previous
import numpy as np
from contextlib import ExitStack
import concourse.bass as bass
import concourse.mybir as mybir
from concourse.bass import IndirectOffsetOnAxis
from concourse.bass_utils import run_bass_kernel_spmd

F32 = mybir.dt.float32
BF16 = mybir.dt.bfloat16
I32 = mybir.dt.int32
U8 = mybir.dt.uint8
AF = mybir.ActivationFunctionType
ALU = mybir.AluOpType
AXX = mybir.AxisListType.X

T = 2048
NT = 16
D = 1024
DC = 8
E = 8
SLOT = 512
RT = SLOT // 128
NH = 16
DH = 64
HF = 4096
NHC = HF // 128
EPS = 1e-5
CAP = 3072.0
BIS_IT = 20
REC = 1.0 / D
NCHUNK = 16
RG8 = [[0, 1, 2, 3, 4, 5, 6, 7]]

DO_ATTN = True
DO_FFN2 = True


class _MockIns:
    __slots__ = ()

    def then_inc(self, *a, **k):
        return self


_MOCK_INS = _MockIns()


class _MockEng:
    def __getattr__(self, name):
        def f(*a, **k):
            return _MOCK_INS
        return f


_MOCK = _MockEng()


class Sem:
    __slots__ = ("h", "n", "w", "name")

    def __init__(self, h, name):
        self.h = h
        self.n = 0
        self.w = {}
        self.name = name

    def snap(self):
        return (self.n, dict(self.w))

    def restore(self, s):
        self.n, self.w = s[0], dict(s[1])


class Ctx:
    pass


def sec(c, ename):
    def deco(fn):
        if c.dry:
            fn(_MOCK)
        else:
            getattr(c.block, ename)(fn)
        return None
    return deco


def inc(c, ins, s, amt, event=None):
    s.n += amt
    if event is not None:
        if c.dry:
            c.events[event] = (s, s.n)
        else:
            assert c.events[event][1] == s.n, (event, c.events[event][1], s.n)
    if not c.dry:
        ins.then_inc(s.h, amt)
    return s.n


def cc_inc(c, ins, event):
    s = c.sems["ccs"]
    s.n += 1
    if c.dry:
        c.events[event] = (s, s.n)
    else:
        ins.then_inc(s.h)
    return s.n


def wait_ev(c, eng, ename, event):
    if c.dry:
        return
    s, v = c.events[event]
    if v <= 0 or v <= s.w.get(ename, 0):
        return
    eng.wait_ge(s.h, v)
    s.w[ename] = v


def vf(c, eng, ename, s, ins):
    v = inc(c, ins, s, 1)
    wait_abs(c, eng, ename, s, v)


def wait_abs(c, eng, ename, s, v):
    if c.dry:
        return
    if v <= 0 or v <= s.w.get(ename, 0):
        return
    eng.wait_ge(s.h, v)
    s.w[ename] = v


def run_dual(c, emitfn):
    snaps = {k: s.snap() for k, s in c.sems.items()}
    c.dry = True
    emitfn()
    for k, s in c.sems.items():
        s.restore(snaps[k])
    c.dry = False
    emitfn()


def make_consts(nc):
    cs = {}
    cs["ident_f32"] = nc.inline_tensor(np.eye(128, dtype=np.float32), "c_ident")
    cs["ltri"] = nc.inline_tensor(np.triu(np.ones((128, 128), np.float32), 1), "c_ltri")
    a64 = (np.arange(64)[:, None] // 8 == np.arange(64)[None, :] // 8).astype(np.float32)
    cs["a64"] = nc.inline_tensor(a64, "c_a64")
    cs["epart"] = nc.inline_tensor((np.arange(64) // 8).astype(np.float32).reshape(64, 1), "c_epart")
    cm = (np.arange(128)[:, None] <= np.arange(128)[None, :]).astype(np.float32)
    cs["causalT"] = nc.inline_tensor(cm, "c_causalT")
    cs["iota8"] = nc.inline_tensor(np.tile(np.arange(8, dtype=np.float32), (128, 1)), "c_iota8")
    tk = np.zeros((128, 32), np.float32)
    for col in range(32):
        tk[:, col] = (col % 16) * 128 + np.arange(128)
    cs["tokids"] = nc.inline_tensor(tk, "c_tokids")
    trash = np.tile((32 * 128 + np.arange(128, dtype=np.float32))[:, None], (1, 32))
    cs["trash"] = nc.inline_tensor(trash, "c_trash")
    cs["ones_f32_row"] = nc.inline_tensor(np.ones((1, 128), np.float32), "c_onesf")
    cs["ones_bf_row"] = nc.inline_tensor(np.ones((1, 2048), np.float32), "c_onesb")
    e2 = (np.arange(16)[:, None] // 2 == np.arange(8)[None, :]).astype(np.float32)
    cs["e2sel"] = nc.inline_tensor(e2, "c_e2sel")
    lh = (np.arange(16)[:, None] % 2 == (np.arange(128)[None, :] // 64)).astype(np.float32)
    cs["lhsel"] = nc.inline_tensor(lh, "c_lhsel")
    return cs


def build_program(debug=()):
    nc = bass.Bass("TRN2", target_bir_lowering=False)
    c = Ctx()
    c.nc = nc
    c.debug = set(debug)
    c.dbg_bufs = {}
    c.events = {}
    c.dry = False

    io = {}
    io["x_in"] = nc.dram_tensor("x_in", [T, D], F32, kind="ExternalInput").ap()
    for nm in ("wqT", "wkT", "wvT", "woT"):
        io[nm] = nc.dram_tensor(nm, [D, D], F32, kind="ExternalInput").ap()
    for nm in ("bq", "bk", "bv", "bo"):
        io[nm] = nc.dram_tensor(nm, [1, D], F32, kind="ExternalInput").ap()
    io["gamma"] = nc.dram_tensor("gamma", [NH, 1], F32, kind="ExternalInput").ap()
    io["rw"] = nc.dram_tensor("rw", [D, E], F32, kind="ExternalInput").ap()
    io["rb"] = nc.dram_tensor("rb", [1, E], F32, kind="ExternalInput").ap()
    io["w1"] = nc.dram_tensor("w1", [E, D, HF], F32, kind="ExternalInput").ap()
    io["b1"] = nc.dram_tensor("b1", [E, HF], F32, kind="ExternalInput").ap()
    io["w2"] = nc.dram_tensor("w2", [E, HF, D], F32, kind="ExternalInput").ap()
    io["b2"] = nc.dram_tensor("b2", [1, E * D], F32, kind="ExternalInput").ap()
    io["lng"] = nc.dram_tensor("lng", [1, D], F32, kind="ExternalInput").ap()
    io["lnb"] = nc.dram_tensor("lnb", [1, D], F32, kind="ExternalInput").ap()
    io["iotaq"] = nc.dram_tensor("iotaq", [128, T], F32, kind="ExternalInput").ap()
    io["wprev"] = nc.dram_tensor("wprev", [128, 8], F32, kind="ExternalInput").ap()
    io["y_out"] = nc.dram_tensor("y_out", [T, D], F32, kind="ExternalOutput").ap()

    io["x_dram"] = nc.dram_tensor("x_dram", [T, D], F32).ap()
    io["xln_dram"] = nc.dram_tensor("xln_dram", [T, D], BF16).ap()
    io["xin_dram"] = nc.dram_tensor("xin_dram", [2, SLOT, D], BF16).ap()
    io["y_dram"] = nc.dram_tensor("y_dram", [E * SLOT, D], BF16).ap()
    io["idx_dram"] = nc.dram_tensor("idx_dram", [33 * 128, 1], F32).ap()
    io["w1bf"] = nc.dram_tensor("w1bf", [E, D, HF], BF16).ap()
    io["w2bf"] = nc.dram_tensor("w2bf", [E, HF, D], BF16).ap()
    io["attn_dram"] = nc.dram_tensor("attn_dram", [T, D], BF16).ap()
    io["ccvi"] = nc.dram_tensor("ccvi", [128, 32], F32).ap()
    io["ccvo"] = nc.dram_tensor("ccvo", [1024, 32], F32).ap()
    io["ccei"] = nc.dram_tensor("ccei", [128, 32], F32).ap()
    io["cceo"] = nc.dram_tensor("cceo", [1024, 32], F32).ap()
    io["ccli"] = nc.dram_tensor("ccli", [NH * DH, DH], F32).ap()
    io["cclo"] = nc.dram_tensor("cclo", [8 * NH * DH, DH], F32).ap()
    c.io = io
    c.cs = make_consts(nc)

    def dbgf(name, shape, dtype):
        if name in c.debug:
            if name not in c.dbg_bufs:
                c.dbg_bufs[name] = nc.dram_tensor("dbg_" + name, list(shape), dtype,
                                                  kind="ExternalOutput").ap()
            return c.dbg_bufs[name]
        return None

    c.dbgf = dbgf

    with ExitStack() as stack:
        c.stack = stack

        def sb(name, shape, dtype):
            return stack.enter_context(nc.sbuf_tensor(name, shape, dtype))

        c.sbf = sb
        c.sems = {}

        def sem(name):
            if name not in c.sems:
                c.sems[name] = Sem(stack.enter_context(nc.semaphore(name)), name)
            return c.sems[name]

        c.sem = sem

        c.ident_f32 = sb("ident_f32", [128, 128], F32)
        c.ltri = sb("ltri", [128, 128], F32)
        c.a64 = sb("a64", [64, 64], F32)
        c.epart = sb("epart", [64, 1], F32)
        c.causalT = sb("causalT", [128, 128], BF16)
        c.iota8 = sb("iota8", [128, 8], F32)
        c.tokids = sb("tokids", [128, 32], F32)
        c.trash = sb("trash", [128, 32], F32)
        c.ones_f32 = sb("ones_f32", [1, 128], F32)
        c.ones_bf = sb("ones_bf", [1, 2048], BF16)
        c.rwsb = sb("rwsb", [128, DC * E], BF16)
        c.rbsb = sb("rbsb", [1, E], BF16)
        c.b1sb = sb("b1sb", [128, E * NHC], F32)
        c.bqsb = sb("bqsb", [128, DC], F32)
        c.bksb = sb("bksb", [128, DC], F32)
        c.bvsb = sb("bvsb", [1, D], BF16)
        c.bosb = sb("bosb", [1, D], BF16)
        c.lngb = sb("lngb", [128, D], F32)
        c.halflnbb = sb("halflnbb", [128, D], F32)
        c.lngam = sb("lngam", [128, DC], F32)
        c.lngrow16 = sb("lngrow16", [1, 16], F32)
        c.wprev_sb = sb("wprev_sb", [128, 8], F32)
        c.zrow = sb("zrow", [33, 128], F32)

        c.xT = sb("xT", [128, DC * T], BF16)
        c.warena = sb("warena", [128, 2 * 8 * 1024], BF16)
        c.hsb = sb("hsb", [128, NHC * SLOT], BF16)
        c.xinT = sb("xinT", [128, 2 * DC * SLOT], BF16)
        c.uarena = sb("uarena", [128, 8192], F32)
        c.xtile = [sb(f"xtile{j}", [128, D], F32) for j in range(2)]
        c.lnx = [sb(f"lnx{j}", [128, D], F32) for j in range(2)]
        c.xlnt = [sb(f"xlnt{j}", [128, D], BF16) for j in range(2)]
        c.scrA = sb("scrA", [128, D], F32)
        c.xg = [c.scrA[:, j * 512:(j + 1) * 512].bitcast(BF16) for j in range(2)]
        c.stat = sb("stat", [128, NT], F32)
        c.stat2 = sb("stat2", [128, NT], F32)
        c.negmu = sb("negmu", [128, NT], F32)
        c.rstd = sb("rstd", [128, NT], F32)
        c.tmpc = sb("tmpc", [128, NT], F32)
        c.tmpc2 = sb("tmpc2", [128, NT], F32)
        c.gbuf = [c.hsb[:, j * 1024:(j + 1) * 1024] for j in range(4)]
        c.ycomb = [c.xinT[:, j * 2048:(j + 1) * 2048].bitcast(F32) for j in range(2)]
        c.ybf = [sb(f"ybf{j}", [128, 512], BF16) for j in range(2)]
        c.outhp = sb("outhp", [128, T], BF16)
        c.asb = [sb(f"asb{j}", [128, 128], BF16) for j in range(2)]
        c.kc = sb("kc", [128, D], BF16)
        c.s0sb = c.uarena[:, 6400:7424]
        c.lsb = c.uarena[0:64, 5376:6400]

        c.Mt = c.uarena[0:64, 0:4096]
        c.eestg = c.uarena[0:64, 4096:6144]
        c.bjunkV = c.uarena[0:64, 4096:8192]
        def bfv(plo, phi, f32lo, f32hi):
            return c.uarena[plo:phi, f32lo:f32hi].bitcast(BF16)

        c.bfv = bfv
        c.stt = c.uarena[:, 2048:2304]
        c.QT0, c.KT0, c.GB0, c.SS0, c.B20, c.BK0 = 2304, 3328, 4352, 0, 6144, 7424
        c.QThp = bfv(0, 128, 2304, 3328)
        c.KThp = bfv(0, 128, 3328, 4352)
        c.gbf = bfv(0, 128, 4352, 5376)
        c.Ssb = bfv(0, 128, 0, 2048)
        c.b2sb = [bfv(0, 1, 6144 + j * 512, 6144 + (j + 1) * 512) for j in range(2)]

        c.logitsT = c.uarena[64:72, 0:2048]
        c.logits = sb("logits", [128, NT * E], F32)
        c.m1 = sb("m1", [128, NT], F32)
        c.m2 = sb("m2", [128, NT], F32)
        c.eqw = sb("eqw", [128, NT * E], F32)
        c.eqw2 = sb("eqw2", [128, NT * E], F32)
        c.vcand = sb("vcand", [128, 32], F32)
        c.ecand = sb("ecand", [128, 32], F32)
        for nm in ("bis_lo", "bis_hi", "bis_mid", "bis_t1", "bis_t2", "bis_pred",
                   "bis_npred", "pcv", "pcp", "pc"):
            setattr(c, nm, sb(nm, [64, 1], F32))
        c.bistrace = sb("bistrace", [64, 3 * BIS_IT], F32)
        c.throw = sb("throw", [1, 64], F32)
        c.thetas = sb("thetas", [128, E], F32)
        c.flag8 = sb("flag8", [128, E * 32], BF16)
        c.scan8 = sb("scan8", [128, E * 32], BF16)
        c.rowtot = sb("rowtot", [128, E], F32)
        c.rowoff = sb("rowoff", [128, E], F32)
        c.gslot = sb("gslot", [128, 32], F32)
        c.yrow_e = sb("yrow_e", [128, 32], F32)
        c.oke = sb("oke", [128, 32], F32)
        c.flagok = sb("flagok", [128, 32], F32)
        c.yrow = sb("yrow", [128, 32], F32)
        c.sacc = sb("sacc", [128, 32], F32)
        c.wv = sb("wv", [128, 32], F32)
        c.sacc_u8 = sb("sacc_u8", [128, 32], U8)
        c.offs_f = sb("offs_f", [128, 32], F32)
        c.offs_i = sb("offs_i", [128, 32], I32)
        c.idxstg = sb("idxstg", [32, 128], F32)
        c.idxf = sb("idxf", [128, 32], F32)
        c.idxt = sb("idxt", [128, 32], I32)
        c.yrow_i = sb("yrow_i", [128, 32], I32)
        c.tmpcand = sb("tmpcand", [128, 32], F32)

        c.b1stg = c.uarena[0:32, 0:1024]
        c.bqstg = c.uarena[0:8, 1024:1152]
        c.bkstg = c.uarena[0:8, 1152:1280]
        c.gmst = sb("gmst", [16, 1], F32)
        c.e2sb = sb("e2sb", [16, 8], F32)
        c.lngam_rhs = sb("lngam_rhs", [16, 8], F32)
        c.lhselsb = sb("lhselsb", [16, 128], F32)
        c.lngrow = c.uarena[0:1, 1280:2304]
        c.lnbrow = c.uarena[0:1, 2304:3328]

        c.block = stack.enter_context(nc.Block())
        for name in ("xld", "xlnst", "xtld", "wld", "wgld", "wbst", "gq", "ccs",
                     "yst", "outst", "vS", "sS", "pS", "tS"):
            sem(name)

        with nc.psum_tensor("psA", [128, 512], F32) as psA, \
             nc.psum_tensor("psB", [128, 512], F32) as psB, \
             nc.psum_tensor("psC", [128, 512], F32) as psC, \
             nc.psum_tensor("psD", [128, 512], F32) as psD, \
             nc.psum_tensor("psK0", [128, 1024], F32) as psK0, \
             nc.psum_tensor("psK1", [128, 1024], F32) as psK1:
            c.ps = [psA, psB, psC, psD]
            c.psK = [psK0, psK1]

            run_dual(c, lambda: emit_init(c))
            run_dual(c, lambda: emit_ffn(c, 0, io["x_in"], io["x_dram"]))
            if DO_ATTN:
                run_dual(c, lambda: emit_attn(c, io["x_dram"], io["x_dram"]))
            if DO_FFN2:
                run_dual(c, lambda: emit_ffn(c, 1, io["x_dram"], io["y_out"]))
            else:
                last = "a_done" if DO_ATTN else "f0_done"
                run_dual(c, lambda: emit_copy(c, io["x_dram"], io["y_out"], last))
            run_dual(c, lambda: emit_final_fence(c))

    return nc, c


def emit_init(c):
    io, cs = c.io, c.cs
    xld, wgld, vS, sS, tS, pS = (c.sem(n) for n in ("xld", "wgld", "vS", "sS", "tS", "pS"))
    psA, psB = c.ps[0], c.ps[1]

    @sec(c, "sync")
    def _(sy):
        for dst, src in (
            (c.ident_f32[:], cs["ident_f32"].ap()), (c.ltri[:], cs["ltri"].ap()),
            (c.a64[:], cs["a64"].ap()), (c.epart[:], cs["epart"].ap()),
            (c.iota8[:], cs["iota8"].ap()), (c.tokids[:], cs["tokids"].ap()),
            (c.trash[:], cs["trash"].ap()), (c.ones_f32[:], cs["ones_f32_row"].ap()),
            (c.wprev_sb[:], io["wprev"]),
            (c.bqstg, io["bq"].rearrange("one (q p) -> (one q) p", q=8)),
            (c.bkstg, io["bk"].rearrange("one (q p) -> (one q) p", q=8)),
            (c.gmst[:], io["gamma"]), (c.e2sb[:], cs["e2sel"].ap()),
            (c.lhselsb[:], cs["lhsel"].ap()),
            (c.lngrow, io["lng"]), (c.lnbrow, io["lnb"]),
        ):
            ins = sy.dma_start(out=dst, in_=src)
            inc(c, ins, xld, 16)
        ins = sy.dma_start(out=c.b1stg.rearrange("q (e p) -> q e p", e=E),
                           in_=io["b1"].rearrange("e (q p) -> q e p", q=32))
        inc(c, ins, xld, 16, "init_ld")

    @sec(c, "gpsimd")
    def _(gp):
        for dst, src in (
            (c.causalT[:], cs["causalT"].ap()), (c.ones_bf[:], cs["ones_bf_row"].ap()),
            (c.rbsb[:], io["rb"]), (c.bvsb[:], io["bv"]), (c.bosb[:], io["bo"]),
        ):
            ins = gp.dma_start(out=dst, in_=src)
            inc(c, ins, wgld, 16)
        ins = gp.dma_start(out=c.rwsb[:].rearrange("p (cc e) -> p cc e", e=E),
                           in_=io["rw"].rearrange("(cc p) e -> p cc e", p=128))
        inc(c, ins, wgld, 16, "init_gld")
        ins = gp.memset(c.zrow[:], 0.0)
        inc(c, ins, pS, 1, "init_zrow")

    @sec(c, "scalar")
    def _(sc):
        wait_ev(c, sc, "scalar", "init_ld")
        ins = sc.activation(out=c.gmst[:], in_=c.gmst[:], func=AF.Ln)
        inc(c, ins, sS, 1, "init_lgam")

    @sec(c, "vector")
    def _(ve):
        wait_ev(c, ve, "vector", "init_lgam")
        ins = ve.tensor_scalar(out=c.lngam_rhs[:], in0=c.e2sb[:], scalar1=c.gmst[:],
                               scalar2=None, op0=ALU.mult)
        inc(c, ins, vS, 1, "init_rhs")

    @sec(c, "tensor")
    def _(te):
        wait_ev(c, te, "tensor", "init_ld")
        wait_ev(c, te, "tensor", "init_rhs")
        for e in range(4):
            te.transpose(psA[:, e * 32:(e + 1) * 32],
                         c.b1stg[:, e * 128:(e + 1) * 128], c.ident_f32[0:32, 0:32])
        for e in range(4, 8):
            te.transpose(psB[:, (e - 4) * 32:(e - 3) * 32],
                         c.b1stg[:, e * 128:(e + 1) * 128], c.ident_f32[0:32, 0:32])
        te.transpose(psA[:, 128:136], c.bqstg, c.ident_f32[0:8, 0:8])
        te.transpose(psA[:, 136:144], c.bkstg, c.ident_f32[0:8, 0:8])
        ins = te.matmul(psA[:, 144:152], c.lhselsb[:], c.lngam_rhs[:], start=True, stop=True)
        inc(c, ins, tS, 1, "init_tr1")

    @sec(c, "vector")
    def _(ve):
        wait_ev(c, ve, "vector", "init_tr1")
        ve.tensor_copy(out=c.b1sb[:, 0:128], in_=psA[:, 0:128])
        ve.tensor_copy(out=c.b1sb[:, 128:256], in_=psB[:, 0:128])
        ve.tensor_copy(out=c.bqsb[:], in_=psA[:, 128:136])
        ve.tensor_copy(out=c.bksb[:], in_=psA[:, 136:144])
        ins = ve.tensor_copy(out=c.lngam[:], in_=psA[:, 144:152])
        inc(c, ins, vS, 1, "init_ev1")

    @sec(c, "tensor")
    def _(te):
        wait_ev(c, te, "tensor", "init_ev1")
        te.matmul(psA[:, 0:512], c.ones_f32[:], c.lngrow[:, 0:512], start=True, stop=True)
        ins = te.matmul(psB[:, 0:512], c.ones_f32[:], c.lngrow[:, 512:1024], start=True, stop=True)
        inc(c, ins, tS, 1, "init_tr2")

    @sec(c, "vector")
    def _(ve):
        wait_ev(c, ve, "vector", "init_tr2")
        ve.tensor_copy(out=c.lngb[:, 0:512], in_=psA[:, 0:512])
        ins = ve.tensor_copy(out=c.lngb[:, 512:1024], in_=psB[:, 0:512])
        inc(c, ins, vS, 1, "init_ev2")

    @sec(c, "tensor")
    def _(te):
        wait_ev(c, te, "tensor", "init_ev2")
        te.matmul(psA[:, 0:512], c.ones_f32[:], c.lnbrow[:, 0:512], start=True, stop=True)
        ins = te.matmul(psB[:, 0:512], c.ones_f32[:], c.lnbrow[:, 512:1024], start=True, stop=True)
        inc(c, ins, tS, 1, "init_tr3")

    @sec(c, "vector")
    def _(ve):
        wait_ev(c, ve, "vector", "init_tr3")
        ve.tensor_scalar(out=c.halflnbb[:, 0:512], in0=psA[:, 0:512], scalar1=0.5,
                         scalar2=None, op0=ALU.mult)
        ins = ve.tensor_scalar(out=c.halflnbb[:, 512:1024], in0=psB[:, 0:512],
                               scalar1=0.5, scalar2=None, op0=ALU.mult)
        inc(c, ins, vS, 1, "init_done")


def emit_ln(c, tag, x_src, wait_before=None, wait_tile=None):
    xld, xlnst, vS, sS = (c.sem(n) for n in ("xld", "xlnst", "vS", "sS"))

    @sec(c, "sync")
    def _(sy):
        if wait_before is not None:
            wait_ev(c, sy, "sync", wait_before)
        for i in range(NT):
            if wait_tile is not None:
                wait_ev(c, sy, "sync", wait_tile.format(i))
            if i >= 2:
                wait_ev(c, sy, "sync", f"{tag}_norm{i - 2}")
                wait_ev(c, sy, "sync", f"{tag}_sq{i - 2}")
            ins = sy.dma_start(out=c.lnx[i % 2][:], in_=x_src[i * 128:(i + 1) * 128, :])
            inc(c, ins, xld, 16, f"{tag}_x{i}")

    @sec(c, "vector")
    def _(ve):
        if wait_before is not None:
            wait_ev(c, ve, "vector", wait_before)
        for i in range(NT):
            wait_ev(c, ve, "vector", f"{tag}_x{i}")
            xt = c.lnx[i % 2]
            ins = ve.reduce_sum(out=c.stat[:, i:i + 1], in_=xt[:], axis=AXX)
            wait_ev(c, ve, "vector", f"{tag}_sq{i}")
            vf(c, ve, "vector", vS, ins)
            ins = ve.tensor_scalar(out=c.negmu[:, i:i + 1], in0=c.stat[:, i:i + 1], scalar1=-REC,
                                   scalar2=None, op0=ALU.mult)
            ve.tensor_scalar(out=c.tmpc[:, i:i + 1], in0=c.stat2[:, i:i + 1], scalar1=REC,
                             scalar2=None, op0=ALU.mult)
            vf(c, ve, "vector", vS, ins)
            ins = ve.tensor_mul(out=c.tmpc2[:, i:i + 1], in0=c.negmu[:, i:i + 1],
                                in1=c.negmu[:, i:i + 1])
            vf(c, ve, "vector", vS, ins)
            ins = ve.tensor_sub(out=c.tmpc[:, i:i + 1], in0=c.tmpc[:, i:i + 1], in1=c.tmpc2[:, i:i + 1])
            vf(c, ve, "vector", vS, ins)
            ins = ve.tensor_scalar(out=c.tmpc[:, i:i + 1], in0=c.tmpc[:, i:i + 1], scalar1=EPS,
                                   scalar2=None, op0=ALU.add)
            inc(c, ins, vS, 1, f"{tag}_var{i}")
            wait_ev(c, ve, "vector", f"{tag}_rstd{i}")
            ins = ve.tensor_scalar(out=c.xlnt[i % 2][:], in0=xt[:], scalar1=c.negmu[:, i:i + 1],
                                   scalar2=c.rstd[:, i:i + 1], op0=ALU.add, op1=ALU.mult)
            inc(c, ins, vS, 1, f"{tag}_norm{i}")

    @sec(c, "scalar")
    def _(sc):
        if wait_before is not None:
            wait_ev(c, sc, "scalar", wait_before)
        for i in range(NT):
            wait_ev(c, sc, "scalar", f"{tag}_x{i}")
            ins = sc.activation(out=c.scrA[:], in_=c.lnx[i % 2][:], func=AF.Square,
                                accum_out=c.stat2[:, i:i + 1])
            inc(c, ins, sS, 1, f"{tag}_sq{i}")
            wait_ev(c, sc, "scalar", f"{tag}_var{i}")
            sc.activation(out=c.tmpc2[:, i:i + 1], in_=c.tmpc[:, i:i + 1], func=AF.Ln)
            ins = sc.activation(out=c.rstd[:, i:i + 1], in_=c.tmpc2[:, i:i + 1], func=AF.Exp,
                                scale=-0.5)
            inc(c, ins, sS, 1, f"{tag}_rstd{i}")
            wait_ev(c, sc, "scalar", f"{tag}_norm{i}")
            ins = sc.dma_start(out=c.io["xln_dram"][i * 128:(i + 1) * 128, :], in_=c.xlnt[i % 2][:])
            inc(c, ins, xlnst, 16, f"{tag}_xln" if i == NT - 1 else f"{tag}_xlnw{i}")


def emit_ffn(c, ph, x_src, x_dst):
    io = c.io
    tag = f"f{ph}"
    (xld, xlnst, xtld, wld, wgld, wbst, gq, ccs, yst, outst, vS, sS, pS, tS) = (
        c.sem(n) for n in ("xld", "xlnst", "xtld", "wld", "wgld", "wbst", "gq", "ccs",
                           "yst", "outst", "vS", "sS", "pS", "tS"))
    psA, psB, psC, psD = c.ps
    psK0, psK1 = c.psK
    NWIN = 12

    if ph == 0:
        emit_ln(c, tag, x_src, wait_before="init_done")
    else:
        emit_ln(c, tag, x_src, wait_tile="a_xot{}")

    def win_slot(widx):
        return widx % 2

    def win_srcs(e, wl, bf):
        if wl < 4:
            w1s = io["w1bf"] if bf else io["w1"]
            out = []
            for dp in range(4):
                src = w1s[e, dp * 256:(dp + 1) * 256, wl * 1024:(wl + 1) * 1024] \
                    .rearrange("(two p) h -> p two h", p=128)
                out.append((dp * 2048, 2048, src))
            return out
        j = wl - 4
        nh, q = divmod(j, 4)
        w2s = io["w2bf"] if bf else io["w2"]
        src = w2s[e, q * 1024:(q + 1) * 1024, nh * 512:(nh + 1) * 512] \
            .rearrange("(hq p) n -> p hq n", p=128)
        return [(0, 4096, src)]

    wsem = wgld if ph == 0 else wld
    wq_name = "gpsimd" if ph == 0 else "sync"

    def emit_wload(eng, e, wl):
        widx = e * NWIN + wl
        slot = win_slot(widx)
        if widx >= 2:
            wait_ev(c, eng, wq_name, f"{tag}_windone{widx - 2}")
            if ph == 0:
                wait_ev(c, eng, wq_name, f"{tag}_wb{widx - 2}")
        base = slot * 8 * 1024
        srcs = win_srcs(e, wl, bf=(ph == 1))
        for k, (off, width, src) in enumerate(srcs):
            nblk = src.shape[1]
            dst = c.warena[:, base + off:base + off + width] \
                .rearrange("p (b w) -> p b w", b=nblk)
            ins = eng.dma_start(out=dst, in_=src)
            inc(c, ins, wsem, 16, f"{tag}_wload{widx}" if k == len(srcs) - 1 else None)

    def emit_wb(eng, e, wl):
        widx = e * NWIN + wl
        base = win_slot(widx) * 8 * 1024
        wait_ev(c, eng, "sync", f"{tag}_wload{widx}")
        if wl < 4:
            for dp in range(4):
                dst = io["w1bf"][e, dp * 256:(dp + 1) * 256, wl * 1024:(wl + 1) * 1024] \
                    .rearrange("(two p) h -> p two h", p=128)
                ins = eng.dma_start(
                    out=dst,
                    in_=c.warena[:, base + dp * 2048:base + (dp + 1) * 2048]
                    .rearrange("p (b w) -> p b w", b=2))
                inc(c, ins, wbst, 16, f"{tag}_wb{widx}" if dp == 3 else None)
        else:
            j = wl - 4
            nh, q = divmod(j, 4)
            dst = io["w2bf"][e, q * 1024:(q + 1) * 1024, nh * 512:(nh + 1) * 512] \
                .rearrange("(hq p) n -> p hq n", p=128)
            ins = eng.dma_start(out=dst,
                                in_=c.warena[:, base:base + 4096]
                                .rearrange("p (b w) -> p b w", b=8))
            inc(c, ins, wbst, 16, f"{tag}_wb{widx}")

    def emit_route_gp(gp):
        wait_ev(c, gp, "gpsimd", f"{tag}_cand")
        ins = gp.dma_start(out=io["ccvi"], in_=c.vcand[:])
        inc(c, ins, gq, 16)
        ins = gp.dma_start(out=io["ccei"], in_=c.ecand[:])
        inc(c, ins, gq, 16, f"{tag}_ccin")
        wait_ev(c, gp, "gpsimd", f"{tag}_ccin")
        ins = gp.collective_compute("AllGather", ALU.bypass, replica_groups=RG8,
                                    ins=[io["ccvi"].opt()], outs=[io["ccvo"].opt()])
        cc_inc(c, ins, f"{tag}_ccv")
        ins = gp.collective_compute("AllGather", ALU.bypass, replica_groups=RG8,
                                    ins=[io["ccei"].opt()], outs=[io["cceo"].opt()])
        cc_inc(c, ins, f"{tag}_cce")
        wait_ev(c, gp, "gpsimd", "init_zrow")
        ins = gp.dma_start(out=io["idx_dram"].rearrange("(b p) one -> b (p one)", b=33),
                           in_=c.zrow[:])
        inc(c, ins, gq, 16, f"{tag}_idxclr")
        wait_ev(c, gp, "gpsimd", f"{tag}_offs")
        wait_ev(c, gp, "gpsimd", f"{tag}_idxclr")
        for col in range(32):
            ins = gp.indirect_dma_start(
                out=io["idx_dram"],
                out_offset=IndirectOffsetOnAxis(c.offs_i[:, col:col + 1], 0),
                in_=c.tokids[:, col:col + 1], in_offset=None)
            inc(c, ins, gq, 16, f"{tag}_scat" if col == 31 else None)

    def emit_gather(gp, e):
        wait_ev(c, gp, "gpsimd", f"{tag}_idxt")
        if e >= 2:
            wait_ev(c, gp, "gpsimd", f"{tag}_xinT{e - 2}")
        for rt in range(RT):
            gi = e * RT + rt
            if gi >= 2:
                wait_ev(c, gp, "gpsimd", f"{tag}_xinw{gi - 2}")
            ins = gp.indirect_dma_start(
                out=c.xg[gi % 2],
                out_offset=None,
                in_=io["xln_dram"],
                in_offset=IndirectOffsetOnAxis(c.idxt[:, e * 4 + rt:e * 4 + rt + 1], 0))
            inc(c, ins, gq, 16, f"{tag}_xing{gi}")
            wait_ev(c, gp, "gpsimd", f"{tag}_xing{gi}")
            ins = gp.dma_start(out=io["xin_dram"][e % 2, rt * 128:(rt + 1) * 128, :],
                               in_=c.xg[gi % 2])
            inc(c, ins, gq, 16,
                f"{tag}_gather{e}" if rt == RT - 1 else f"{tag}_xinw{gi}")
        if True:
            alias_ev(c, f"{tag}_xinw{e * RT + RT - 1}", f"{tag}_gather{e}")
        if e >= 2:
            wait_ev(c, gp, "gpsimd", f"{tag}_w2done{e - 2}")
        ins = gp.dma_start(out=c.b2sb[e % 2], in_=io["b2"][:, e * D:(e + 1) * D])
        inc(c, ins, wgld, 16, f"{tag}_b2{e}")

    def emit_comb_gathers(gp):
        for i in range(NT):
            wait_ev(c, gp, "gpsimd", f"{tag}_ydone")
            wait_ev(c, gp, "gpsimd", f"{tag}_yrowi")
            if i >= 2:
                wait_ev(c, gp, "gpsimd", f"{tag}_comb{i - 2}")
            ins = gp.indirect_dma_start(
                out=c.gbuf[(2 * i) % 4], out_offset=None, in_=io["y_dram"],
                in_offset=IndirectOffsetOnAxis(c.yrow_i[:, i:i + 1], 0))
            inc(c, ins, gq, 16)
            ins = gp.indirect_dma_start(
                out=c.gbuf[(2 * i + 1) % 4], out_offset=None, in_=io["y_dram"],
                in_offset=IndirectOffsetOnAxis(c.yrow_i[:, 16 + i:17 + i], 0))
            inc(c, ins, gq, 16, f"{tag}_g{i}")

    if ph == 0:
        @sec(c, "gpsimd")
        def _(gp):
            emit_wload(gp, 0, 0)
            emit_wload(gp, 0, 1)
            emit_route_gp(gp)
            emit_gather(gp, 0)
            for wl in range(2, NWIN):
                emit_wload(gp, 0, wl)
            for e in range(1, E):
                emit_gather(gp, e)
                for wl in range(NWIN):
                    emit_wload(gp, e, wl)
            emit_comb_gathers(gp)
    else:
        @sec(c, "gpsimd")
        def _(gp):
            emit_route_gp(gp)
            for e in range(E):
                emit_gather(gp, e)
            emit_comb_gathers(gp)

    @sec(c, "sync")
    def _(sy):
        if ph == 1:
            emit_wload(sy, 0, 0)
            emit_wload(sy, 0, 1)
        wait_ev(c, sy, "sync", f"{tag}_xln")
        for dcc in range(DC):
            ins = sy.dma_start(out=c.xT[:, dcc * T:(dcc + 1) * T],
                               in_=io["xln_dram"][:, dcc * 128:(dcc + 1) * 128],
                               transpose=True)
            inc(c, ins, xtld, 16, f"{tag}_xT" if dcc == DC - 1 else None)
        wait_ev(c, sy, "sync", f"{tag}_ccv")
        wait_ev(c, sy, "sync", f"{tag}_cce")
        ccv8 = io["ccvo"].rearrange("(cc x) w -> cc (x w)", cc=8)
        cce8 = io["cceo"].rearrange("(cc x) w -> cc (x w)", cc=8)
        for half in range(2):
            if half == 1:
                wait_ev(c, sy, "sync", f"{tag}_bM0")
            for e in range(E):
                ins = sy.dma_start(
                    out=c.Mt[e * 8:(e + 1) * 8, half * 2048:(half + 1) * 2048],
                    in_=ccv8[:, half * 2048:(half + 1) * 2048])
                inc(c, ins, xtld, 16)
                ins = sy.dma_start(
                    out=c.eestg[e * 8:(e + 1) * 8, :],
                    in_=cce8[:, half * 2048:(half + 1) * 2048])
                inc(c, ins, xtld, 16, f"{tag}_mstg{half}" if e == E - 1 else None)
        wait_ev(c, sy, "sync", f"{tag}_scat")
        ins = sy.dma_start(out=c.idxstg[:].rearrange("b (p one) -> b p one", one=1),
                           in_=io["idx_dram"][0:32 * 128, :].rearrange("(b p) one -> b p one", b=32))
        inc(c, ins, xtld, 16, f"{tag}_idxraw")
        for e in range(E):
            wait_ev(c, sy, "sync", f"{tag}_gather{e}")
            if e >= 2:
                wait_ev(c, sy, "sync", f"{tag}_xincons{e - 2}")
            par = e % 2
            for dcc in range(DC):
                ins = sy.dma_start(
                    out=c.xinT[:, par * DC * SLOT + dcc * SLOT:par * DC * SLOT + (dcc + 1) * SLOT],
                    in_=io["xin_dram"][par, :, dcc * 128:(dcc + 1) * 128],
                    transpose=True)
                inc(c, ins, xtld, 16, f"{tag}_xinT{e}" if dcc == DC - 1 else None)
            if ph == 0:
                for wl in range(NWIN):
                    emit_wb(sy, e, wl)
            else:
                for wl in range(NWIN):
                    if e == 0 and wl < 2:
                        continue
                    emit_wload(sy, e, wl)
        for i in range(NT):
            if i >= 2:
                wait_ev(c, sy, "sync", f"{tag}_xo{i - 2}")
            ins = sy.dma_start(out=c.xtile[i % 2][:], in_=x_src[i * 128:(i + 1) * 128, :])
            inc(c, ins, xld, 16, f"{tag}_cx{i}")

    @sec(c, "tensor")
    def _(te):
        wait_ev(c, te, "tensor", f"{tag}_xT")
        wait_ev(c, te, "tensor", "init_gld")
        wait_ev(c, te, "tensor", "init_ld")
        for g in range(4):
            if g >= 1:
                wait_ev(c, te, "tensor", f"{tag}_lgev{g - 1}")
            for dcc in range(DC):
                te.matmul(psA[0:8, 0:512], c.rwsb[:, dcc * E:(dcc + 1) * E],
                          c.xT[:, dcc * T + g * 512: dcc * T + (g + 1) * 512],
                          start=(dcc == 0), stop=False)
            ins = te.matmul(psA[0:8, 0:512], c.rbsb[:], c.ones_bf[0:1, 0:512],
                            start=False, stop=True)
            inc(c, ins, tS, 1, f"{tag}_lgmm{g}")
        for i in range(NT):
            if i < 2:
                wait_ev(c, te, "tensor", f"{tag}_lgev3")
            else:
                wait_ev(c, te, "tensor", f"{tag}_lgtev{i - 2}")
            pb = psB if i % 2 == 0 else psC
            ins = te.transpose(pb[:, 0:8], c.logitsT[:, i * 128:(i + 1) * 128],
                               c.ident_f32[64:72, 64:72])
            inc(c, ins, tS, 1, f"{tag}_lgt{i}")

    @sec(c, "vector")
    def _(ve):
        for g in range(4):
            wait_ev(c, ve, "vector", f"{tag}_lgmm{g}")
            ins = ve.tensor_copy(out=c.logitsT[:, g * 512:(g + 1) * 512], in_=psA[0:8, 0:512])
            inc(c, ins, vS, 1, f"{tag}_lgev{g}")
        for i in range(NT):
            wait_ev(c, ve, "vector", f"{tag}_lgt{i}")
            pb = psB if i % 2 == 0 else psC
            ins = ve.tensor_copy(out=c.logits[:, i * 8:(i + 1) * 8], in_=pb[:, 0:8])
            inc(c, ins, vS, 1, f"{tag}_lgtev{i}")
        lv = c.logits[:].rearrange("p (t e) -> p t e", e=E)
        eqv = c.eqw[:].rearrange("p (t e) -> p t e", e=E)
        eq2v = c.eqw2[:].rearrange("p (t e) -> p t e", e=E)
        i8 = c.iota8[:].unsqueeze(1).broadcast_to([128, NT, E])
        ins = ve.tensor_reduce(out=c.m1[:].unsqueeze(2), in_=lv, axis=AXX, op=ALU.max)
        vf(c, ve, "vector", vS, ins)
        ins = ve.tensor_tensor(out=eqv, in0=lv,
                               in1=c.m1[:].unsqueeze(2).broadcast_to([128, NT, E]), op=ALU.is_equal)
        vf(c, ve, "vector", vS, ins)
        ins = ve.tensor_tensor(out=eq2v, in0=eqv, in1=i8, op=ALU.mult)
        vf(c, ve, "vector", vS, ins)
        ins = ve.tensor_scalar(out=c.eqw[:], in0=c.eqw[:], scalar1=-99.0, scalar2=99.0,
                               op0=ALU.mult, op1=ALU.add)
        vf(c, ve, "vector", vS, ins)
        ins = ve.tensor_add(out=c.eqw2[:], in0=c.eqw2[:], in1=c.eqw[:])
        vf(c, ve, "vector", vS, ins)
        ins = ve.tensor_reduce(out=c.ecand[:, 0:16].unsqueeze(2), in_=eq2v, axis=AXX, op=ALU.min)
        vf(c, ve, "vector", vS, ins)
        ins = ve.tensor_tensor(out=eqv, in0=i8,
                               in1=c.ecand[:, 0:16].unsqueeze(2).broadcast_to([128, NT, E]),
                               op=ALU.is_equal)
        vf(c, ve, "vector", vS, ins)
        ins = ve.scalar_tensor_tensor(out=eq2v, in0=eqv, scalar=-1e30, in1=lv,
                                      op0=ALU.mult, op1=ALU.add)
        vf(c, ve, "vector", vS, ins)
        ins = ve.tensor_reduce(out=c.m2[:].unsqueeze(2), in_=eq2v, axis=AXX, op=ALU.max)
        vf(c, ve, "vector", vS, ins)
        ins = ve.tensor_tensor(out=eqv, in0=eq2v,
                               in1=c.m2[:].unsqueeze(2).broadcast_to([128, NT, E]), op=ALU.is_equal)
        vf(c, ve, "vector", vS, ins)
        ins = ve.tensor_tensor(out=eq2v, in0=eqv, in1=i8, op=ALU.mult)
        vf(c, ve, "vector", vS, ins)
        ins = ve.tensor_scalar(out=c.eqw[:], in0=c.eqw[:], scalar1=-99.0, scalar2=99.0,
                               op0=ALU.mult, op1=ALU.add)
        vf(c, ve, "vector", vS, ins)
        ins = ve.tensor_add(out=c.eqw2[:], in0=c.eqw2[:], in1=c.eqw[:])
        vf(c, ve, "vector", vS, ins)
        ins = ve.tensor_reduce(out=c.ecand[:, 16:32].unsqueeze(2), in_=eq2v, axis=AXX, op=ALU.min)
        vf(c, ve, "vector", vS, ins)
        ins = ve.tensor_sub(out=c.tmpc[:], in0=c.m1[:], in1=c.m2[:])
        inc(c, ins, vS, 1, f"{tag}_m12")
        wait_ev(c, ve, "vector", f"{tag}_sig")
        ins = ve.tensor_scalar(out=c.vcand[:, 16:32], in0=c.vcand[:, 0:16], scalar1=-1.0,
                               scalar2=1.0, op0=ALU.mult, op1=ALU.add)
        inc(c, ins, vS, 1, f"{tag}_cand")

    @sec(c, "scalar")
    def _(sc):
        wait_ev(c, sc, "scalar", f"{tag}_m12")
        ins = sc.activation(out=c.vcand[:, 0:16], in_=c.tmpc[:], func=AF.Sigmoid)
        inc(c, ins, sS, 1, f"{tag}_sig")

    @sec(c, "vector")
    def _(ve):
        ve.memset(c.bis_lo[:], 0.0)
        ve.memset(c.bis_hi[:], 1.0)
        ve.memset(c.bis_mid[:], 0.5)
        for half in range(2):
            wait_ev(c, ve, "vector", f"{tag}_mstg{half}")
            ins = ve.scalar_tensor_tensor(
                out=c.Mt[:, half * 2048:(half + 1) * 2048], in0=c.eestg,
                scalar=c.epart[:], in1=c.Mt[:, half * 2048:(half + 1) * 2048],
                op0=ALU.is_equal, op1=ALU.mult)
            inc(c, ins, vS, 1, f"{tag}_bM{half}")
            wait_abs(c, ve, "vector", vS, vS.n)
        for it in range(BIS_IT):
            sk = 2.0 ** (-(it + 2))
            ins = ve.tensor_scalar(out=c.bjunkV, in0=c.Mt[:, 0:4096],
                                   scalar1=c.bis_mid[:], scalar2=None,
                                   op0=ALU.is_gt, op1=ALU.add, accum_out=c.pc[:])
            inc(c, ins, vS, 1, f"{tag}_bc{it}")
            wait_ev(c, ve, "vector", f"{tag}_bt{it}")
            ins = ve.tensor_scalar(out=c.bis_pred[:], in0=psD[0:64, 0:1], scalar1=CAP,
                                   scalar2=2.0 * sk, op0=ALU.is_gt, op1=ALU.mult)
            ve.tensor_scalar(out=c.bis_t1[:], in0=c.bis_mid[:], scalar1=-sk, scalar2=None,
                             op0=ALU.add)
            vf(c, ve, "vector", vS, ins)
            ins = ve.tensor_add(out=c.bis_mid[:], in0=c.bis_t1[:], in1=c.bis_pred[:])
            inc(c, ins, vS, 1, f"{tag}_bu{it}")
            wait_abs(c, ve, "vector", vS, vS.n)
        ins = ve.tensor_scalar(out=c.bis_hi[:], in0=c.bis_mid[:], scalar1=2.0 ** (-BIS_IT),
                               scalar2=None, op0=ALU.add)
        inc(c, ins, vS, 1, f"{tag}_bisdone")
        wait_abs(c, ve, "vector", vS, vS.n)

    @sec(c, "tensor")
    def _(te):
        for it in range(BIS_IT):
            wait_ev(c, te, "tensor", f"{tag}_bc{it}")
            ins = te.matmul(psD[0:64, 0:1], c.a64[:], c.pc[:], start=True, stop=True)
            inc(c, ins, tS, 1, f"{tag}_bt{it}")
        wait_ev(c, te, "tensor", f"{tag}_bisdone")
        ins = te.transpose(psK0[0:1, 0:64], c.bis_hi[:], c.ident_f32[0:64, 0:64])
        inc(c, ins, tS, 1, f"{tag}_throwmm")
        wait_ev(c, te, "tensor", f"{tag}_throwev")
        ins = te.matmul(psA[:, 0:8], c.ones_f32[:], c.throw[:, 0:64:8], start=True, stop=True)
        inc(c, ins, tS, 1, f"{tag}_thbc")
        wait_ev(c, te, "tensor", f"{tag}_rowtot")
        ins = te.matmul(psD[:, 8:16], c.ltri[:], c.rowtot[:], start=True, stop=True)
        inc(c, ins, tS, 1, f"{tag}_rooffmm")
        wait_ev(c, te, "tensor", f"{tag}_idxraw")
        ins = te.transpose(psB[:, 0:32], c.idxstg[:], c.ident_f32[0:32, 0:32])
        inc(c, ins, tS, 1, f"{tag}_idxmm")

    @sec(c, "vector")
    def _(ve):
        wait_ev(c, ve, "vector", f"{tag}_throwmm")
        ins = ve.tensor_copy(out=c.throw[:], in_=psK0[0:1, 0:64])
        inc(c, ins, vS, 1, f"{tag}_throwev")
        wait_ev(c, ve, "vector", f"{tag}_thbc")
        ins = ve.tensor_copy(out=c.thetas[:], in_=psA[:, 0:8])
        inc(c, ins, vS, 1, f"{tag}_thetas")
        wait_abs(c, ve, "vector", vS, vS.n)
        for e in range(E):
            fe = c.flag8[:, e * 32:(e + 1) * 32]
            se = c.scan8[:, e * 32:(e + 1) * 32]
            ins = ve.tensor_scalar(out=fe, in0=c.vcand[:], scalar1=c.thetas[:, e:e + 1],
                                   scalar2=None, op0=ALU.is_gt)
            ve.tensor_scalar(out=c.tmpcand[:], in0=c.ecand[:], scalar1=float(e),
                             scalar2=None, op0=ALU.is_equal)
            vf(c, ve, "vector", vS, ins)
            ins = ve.tensor_mul(out=fe, in0=fe, in1=c.tmpcand[:])
            vf(c, ve, "vector", vS, ins)
            ins = ve.tensor_tensor_scan(out=se, data0=fe, data1=fe, initial=0.0,
                                        op0=ALU.add, op1=ALU.bypass)
            vf(c, ve, "vector", vS, ins)
            ins = ve.tensor_copy(out=c.rowtot[:, e:e + 1], in_=se[:, 31:32])
            if e == E - 1:
                inc(c, ins, vS, 1, f"{tag}_rowtot")
                wait_abs(c, ve, "vector", vS, vS.n)
        wait_ev(c, ve, "vector", f"{tag}_rooffmm")
        ins = ve.tensor_copy(out=c.rowoff[:], in_=psD[:, 8:16])
        inc(c, ins, vS, 1, f"{tag}_rooev")
        wait_abs(c, ve, "vector", vS, vS.n)
        ve.memset(c.yrow[:], 0.0)
        ve.memset(c.sacc[:], 0.0)
        for e in range(E):
            fe = c.flag8[:, e * 32:(e + 1) * 32]
            se = c.scan8[:, e * 32:(e + 1) * 32]
            ins = ve.tensor_sub(out=se, in0=se, in1=fe)
            vf(c, ve, "vector", vS, ins)
            ins = ve.tensor_scalar(out=c.gslot[:], in0=se, scalar1=c.rowoff[:, e:e + 1],
                                   scalar2=None, op0=ALU.add)
            vf(c, ve, "vector", vS, ins)
            ins = ve.tensor_scalar(out=c.oke[:], in0=c.gslot[:], scalar1=float(SLOT),
                                   scalar2=None, op0=ALU.is_lt)
            vf(c, ve, "vector", vS, ins)
            ins = ve.tensor_mul(out=c.flagok[:], in0=fe, in1=c.oke[:])
            ve.tensor_scalar(out=c.yrow_e[:], in0=c.gslot[:], scalar1=float(e * SLOT),
                             scalar2=None, op0=ALU.add)
            vf(c, ve, "vector", vS, ins)
            ins = ve.tensor_mul(out=c.yrow_e[:], in0=c.yrow_e[:], in1=c.flagok[:])
            vf(c, ve, "vector", vS, ins)
            ins = ve.tensor_add(out=c.yrow[:], in0=c.yrow[:], in1=c.yrow_e[:])
            ve.tensor_add(out=c.sacc[:], in0=c.sacc[:], in1=c.flagok[:])
            vf(c, ve, "vector", vS, ins)
        ins = ve.tensor_mul(out=c.wv[:], in0=c.vcand[:], in1=c.sacc[:])
        ve.tensor_copy(out=c.sacc_u8[:], in_=c.sacc[:])
        vf(c, ve, "vector", vS, ins)
        ins = ve.select(out=c.offs_f[:], mask=c.sacc_u8[:], on_true=c.yrow[:], on_false=c.trash[:])
        vf(c, ve, "vector", vS, ins)
        ins = ve.tensor_copy(out=c.offs_i[:], in_=c.offs_f[:])
        inc(c, ins, vS, 1, f"{tag}_offs")
        ins = ve.tensor_copy(out=c.yrow_i[:], in_=c.yrow[:])
        inc(c, ins, vS, 1, f"{tag}_yrowi")
        wait_ev(c, ve, "vector", f"{tag}_idxmm")
        ins = ve.tensor_copy(out=c.idxf[:], in_=psB[:, 0:32])
        vf(c, ve, "vector", vS, ins)
        ins = ve.tensor_copy(out=c.idxt[:], in_=c.idxf[:])
        inc(c, ins, vS, 1, f"{tag}_idxt")

    @sec(c, "tensor")
    def _(te):
        for e in range(E):
            par = e % 2
            xv = c.xinT[:, par * DC * SLOT:(par + 1) * DC * SLOT]
            wait_ev(c, te, "tensor", f"{tag}_xinT{e}")
            if e > 0:
                wait_ev(c, te, "tensor", f"{tag}_yevvdone{e - 1}")
            for q in range(4):
                widx = e * NWIN + q
                wait_ev(c, te, "tensor", f"{tag}_wload{widx}")
                base = win_slot(widx) * 8 * 1024
                for hcl in range(8):
                    hc = q * 8 + hcl
                    ghc = e * NHC + hc
                    pb = psA if hc % 2 == 0 else psB
                    if ghc >= 2:
                        wait_ev(c, te, "tensor", f"{tag}_gelu{ghc - 2}")
                    for dcc in range(DC):
                        dp, two = divmod(dcc, 2)
                        lo = base + dp * 2048 + two * 1024 + hcl * 128
                        ins = te.matmul(pb[:, 0:512], c.warena[:, lo:lo + 128],
                                        xv[:, dcc * SLOT:(dcc + 1) * SLOT],
                                        start=(dcc == 0), stop=(dcc == DC - 1))
                    inc(c, ins, tS, 1, f"{tag}_hmm{ghc}")
                alias_ev(c, f"{tag}_windone{widx}", f"{tag}_hmm{e * NHC + q * 8 + 7}")
            wait_ev(c, te, "tensor", f"{tag}_geludone{e}")
            wait_ev(c, te, "tensor", f"{tag}_b2{e}")
            for nh in range(2):
                for q in range(4):
                    widx = e * NWIN + 4 + nh * 4 + q
                    wait_ev(c, te, "tensor", f"{tag}_wload{widx}")
                    base = win_slot(widx) * 8 * 1024
                    for rt in range(RT):
                        gj = e * 8 + nh * RT + rt
                        pb = c.ps[rt]
                        if q == 0 and gj >= 4:
                            wait_ev(c, te, "tensor", f"{tag}_yevv{gj - 4}")
                        for hcl in range(8):
                            hcg = q * 8 + hcl
                            lhs = c.hsb[:, hcg * SLOT + rt * 128: hcg * SLOT + rt * 128 + 128]
                            ins = te.matmul(pb[:, 0:512], lhs,
                                            c.warena[:, base + hcl * 512:base + (hcl + 1) * 512],
                                            start=(q == 0 and hcl == 0), stop=False,
                                            skip_group_check=True)
                        if rt == RT - 1:
                            inc(c, ins, tS, 1, f"{tag}_w2used{e}_{nh}_{q}")
                            alias_ev(c, f"{tag}_windone{widx}", f"{tag}_w2used{e}_{nh}_{q}")
                for rt in range(RT):
                    gj = e * 8 + nh * RT + rt
                    ins = te.matmul(c.ps[rt][:, 0:512], c.ones_bf[0:1, 0:128],
                                    c.bfv(0, 1, c.B20 + (e % 2) * 512 + nh * 256, c.B20 + (e % 2) * 512 + (nh + 1) * 256),
                                    start=False, stop=True, skip_group_check=True)
                    inc(c, ins, tS, 1, f"{tag}_ymm{gj}")
            alias_ev(c, f"{tag}_w2done{e}", f"{tag}_ymm{e * 8 + 7}")
            alias_ev(c, f"{tag}_xincons{e}", f"{tag}_hmm{e * NHC + 31}")

    @sec(c, "scalar")
    def _(sc):
        for e in range(E):
            for hc in range(NHC):
                ghc = e * NHC + hc
                wait_ev(c, sc, "scalar", f"{tag}_hmm{ghc}")
                if e > 0 and hc == 0:
                    wait_ev(c, sc, "scalar", f"{tag}_w2done{e - 1}")
                pb = psA if hc % 2 == 0 else psB
                ins = sc.activation(out=c.hsb[:, hc * SLOT:(hc + 1) * SLOT], in_=pb[:, 0:512],
                                    func=AF.Gelu, bias=c.b1sb[:, e * NHC + hc:e * NHC + hc + 1])
                inc(c, ins, sS, 1, f"{tag}_gelu{ghc}")
            alias_ev(c, f"{tag}_geludone{e}", f"{tag}_gelu{e * NHC + 31}")
            for j in range(2 * RT):
                gj = e * 8 + j
                nh, rt = divmod(j, RT)
                wait_ev(c, sc, "scalar", f"{tag}_yevv{gj}")
                ins = sc.dma_start(
                    out=io["y_dram"][e * SLOT + rt * 128:e * SLOT + (rt + 1) * 128,
                                     nh * 512:(nh + 1) * 512],
                    in_=c.ybf[gj % 2][:])
                inc(c, ins, yst, 16, f"{tag}_ywr{gj}")
            alias_ev(c, f"{tag}_ydone{e}", f"{tag}_ywr{e * 8 + 7}")
        alias_ev(c, f"{tag}_ydone", f"{tag}_ywr{E * 8 - 1}")

    @sec(c, "vector")
    def _(ve):
        for e in range(E):
            for j in range(2 * RT):
                gj = e * 8 + j
                rt = j % RT
                wait_ev(c, ve, "vector", f"{tag}_ymm{gj}")
                if gj >= 2:
                    wait_ev(c, ve, "vector", f"{tag}_ywr{gj - 2}")
                ins = ve.tensor_copy(out=c.ybf[gj % 2][:], in_=c.ps[rt][:, 0:512])
                inc(c, ins, vS, 1, f"{tag}_yevv{gj}")
            alias_ev(c, f"{tag}_yevvdone{e}", f"{tag}_yevv{e * 8 + 7}")

    @sec(c, "vector")
    def _(ve):
        for i in range(NT):
            wait_ev(c, ve, "vector", f"{tag}_g{i}")
            g0 = c.gbuf[(2 * i) % 4]
            g1 = c.gbuf[(2 * i + 1) % 4]
            yc = c.ycomb[i % 2]
            if i >= 2:
                wait_ev(c, ve, "vector", f"{tag}_csq{i - 2}")
            ins = ve.tensor_scalar(out=yc, in0=g0, scalar1=c.wv[:, i:i + 1],
                                   scalar2=None, op0=ALU.mult)
            vf(c, ve, "vector", vS, ins)
            ins = ve.scalar_tensor_tensor(out=yc, in0=g1, scalar=c.wv[:, 16 + i:17 + i],
                                          in1=yc, op0=ALU.mult, op1=ALU.add,
                                          accum_out=c.stat[:, i:i + 1])
            inc(c, ins, vS, 1, f"{tag}_comb{i}")
            wait_abs(c, ve, "vector", vS, vS.n)
            wait_ev(c, ve, "vector", f"{tag}_csq{i}")
            ins = ve.tensor_scalar(out=c.negmu[:, i:i + 1], in0=c.stat[:, i:i + 1], scalar1=-REC,
                                   scalar2=None, op0=ALU.mult)
            ve.tensor_scalar(out=c.tmpc[:, i:i + 1], in0=c.stat2[:, i:i + 1], scalar1=REC,
                             scalar2=None, op0=ALU.mult)
            vf(c, ve, "vector", vS, ins)
            ins = ve.tensor_mul(out=c.tmpc2[:, i:i + 1], in0=c.negmu[:, i:i + 1],
                                in1=c.negmu[:, i:i + 1])
            vf(c, ve, "vector", vS, ins)
            ins = ve.tensor_sub(out=c.tmpc[:, i:i + 1], in0=c.tmpc[:, i:i + 1],
                                in1=c.tmpc2[:, i:i + 1])
            vf(c, ve, "vector", vS, ins)
            ins = ve.tensor_scalar(out=c.tmpc[:, i:i + 1], in0=c.tmpc[:, i:i + 1], scalar1=EPS,
                                   scalar2=None, op0=ALU.add)
            inc(c, ins, vS, 1, f"{tag}_cvar{i}")
            wait_ev(c, ve, "vector", f"{tag}_crstd{i}")
            ins = ve.tensor_scalar(out=yc, in0=yc, scalar1=c.negmu[:, i:i + 1],
                                   scalar2=c.rstd[:, i:i + 1], op0=ALU.add, op1=ALU.mult)
            vf(c, ve, "vector", vS, ins)
            ins = ve.tensor_mul(out=yc, in0=yc, in1=c.lngb[:])
            vf(c, ve, "vector", vS, ins)
            ins = ve.scalar_tensor_tensor(out=yc, in0=yc, scalar=0.5, in1=c.halflnbb[:],
                                          op0=ALU.mult, op1=ALU.add)
            vf(c, ve, "vector", vS, ins)
            wait_ev(c, ve, "vector", f"{tag}_cx{i}")
            ins = ve.tensor_add(out=c.xtile[i % 2][:], in0=c.xtile[i % 2][:], in1=yc)
            inc(c, ins, vS, 1, f"{tag}_cfin{i}")

    @sec(c, "scalar")
    def _(sc):
        for i in range(NT):
            wait_ev(c, sc, "scalar", f"{tag}_comb{i}")
            ins = sc.activation(out=c.scrA[:], in_=c.ycomb[i % 2], func=AF.Square,
                                accum_out=c.stat2[:, i:i + 1])
            inc(c, ins, sS, 1, f"{tag}_csq{i}")
            wait_ev(c, sc, "scalar", f"{tag}_cvar{i}")
            sc.activation(out=c.tmpc2[:, i:i + 1], in_=c.tmpc[:, i:i + 1], func=AF.Ln)
            ins = sc.activation(out=c.rstd[:, i:i + 1], in_=c.tmpc2[:, i:i + 1], func=AF.Exp,
                                scale=-0.5)
            inc(c, ins, sS, 1, f"{tag}_crstd{i}")
            wait_ev(c, sc, "scalar", f"{tag}_cfin{i}")
            ins = sc.dma_start(out=x_dst[i * 128:(i + 1) * 128, :], in_=c.xtile[i % 2][:])
            inc(c, ins, outst, 16, f"{tag}_xo{i}" if i < NT - 1 else f"{tag}_done")
            alias_ev(c, f"{tag}_xot{i}", f"{tag}_xo{i}" if i < NT - 1 else f"{tag}_done")

    if c.debug and ph == 0:
        emit_ffn_debug(c, tag)




def alias_ev(c, new, old):
    if c.dry:
        c.events[new] = c.events[old]


def emit_attn_debug(c):
    io = c.io
    outst = c.sem("outst")

    @sec(c, "sync")
    def _(sy):
        ap = c.dbgf("attnd", [T, D], BF16)
        if ap is not None:
            wait_ev(c, sy, "sync", "a_adw7")
            wait_ev(c, sy, "sync", "a_done")
            for blk in range(0, T, 1024):
                ins = sy.dma_start(out=ap[blk:blk + 1024, :], in_=io["attn_dram"][blk:blk + 1024, :])
                inc(c, ins, outst, 16)
        ap = c.dbgf("ssb", [128, 2048], F32)
        if ap is not None:
            wait_ev(c, sy, "sync", "a_done")
            ins = sy.dma_start(out=ap, in_=c.uarena[:, 0:2048])
            inc(c, ins, outst, 16)
        ap = c.dbgf("s0", [128, 1024], F32)
        if ap is not None:
            wait_ev(c, sy, "sync", "a_done")
            ins = sy.dma_start(out=ap, in_=c.s0sb)
            inc(c, ins, outst, 16)
        ap = c.dbgf("vsbd", [128, NT * D], BF16)
        if ap is not None:
            wait_ev(c, sy, "sync", "a_done")
            ins = sy.dma_start(out=ap, in_=c.hsb[:])
            inc(c, ins, outst, 16)
        ap = c.dbgf("lsbd", [64, 1024], F32)
        if ap is not None:
            wait_ev(c, sy, "sync", "a_done")
            ins = sy.dma_start(out=ap, in_=c.lsb)
            inc(c, ins, outst, 16)
        for nm, src in (("qthp", c.QThp), ("kthp", c.KThp), ("gbfd", c.gbf)):
            ap = c.dbgf(nm, [128, T], BF16)
            if ap is not None:
                wait_ev(c, sy, "sync", "a_done")
                ins = sy.dma_start(out=ap, in_=src)
                inc(c, ins, outst, 16)
        ap = c.dbgf("outhpd", [128, T], BF16)
        if ap is not None:
            wait_ev(c, sy, "sync", "a_done")
            ins = sy.dma_start(out=ap, in_=c.outhp[:])
            inc(c, ins, outst, 16)


def emit_ffn_debug(c, tag):
    io = c.io
    outst = c.sem("outst")
    taps_sb = [
        ("logits", c.logits, [128, NT * E], F32, f"{tag}_cand"),
        ("vcand", c.vcand, [128, 32], F32, f"{tag}_cand"),
        ("ecand", c.ecand, [128, 32], F32, f"{tag}_cand"),
        ("theta", c.thetas, [128, E], F32, f"{tag}_thetas"),
        ("yrow", c.yrow, [128, 32], F32, f"{tag}_yrowi"),
        ("sacc", c.sacc, [128, 32], F32, f"{tag}_yrowi"),
        ("wv", c.wv, [128, 32], F32, f"{tag}_yrowi"),
        ("idxf", c.idxf, [128, 32], F32, f"{tag}_idxt"),
        ("rowtot", c.rowtot, [128, E], F32, f"{tag}_rooev"),
        ("rowoff", c.rowoff, [128, E], F32, f"{tag}_rooev"),
        ("hi", c.bis_hi, [64, 1], F32, f"{tag}_bisdone"),
        ("bist", c.bistrace, [64, 3 * BIS_IT], F32, f"{tag}_bisdone"),
    ]

    @sec(c, "sync")
    def _(sy):
        for name, src, shape, dt_, ev in taps_sb:
            ap = c.dbgf(name, shape, dt_)
            if ap is None:
                continue
            wait_ev(c, sy, "sync", ev)
            wait_ev(c, sy, "sync", f"{tag}_done")
            try:
                srcap = src[:]
            except Exception:
                srcap = src
            ins = sy.dma_start(out=ap, in_=srcap)
            inc(c, ins, outst, 16)
        for name, src, shape, ev in (
            ("xln", io["xln_dram"], [T, D], f"{tag}_xln"),
            ("ydram", io["y_dram"], [E * SLOT, D], f"{tag}_ydone"),
            ("idxdram", io["idx_dram"], [33 * 128, 1], f"{tag}_scat"),
        ):
            ap = c.dbgf(name, shape, BF16 if name in ("xln", "ydram") else F32)
            if ap is None:
                continue
            wait_ev(c, sy, "sync", ev)
            wait_ev(c, sy, "sync", f"{tag}_done")
            for blk in range(0, shape[0], 1024):
                hi_ = min(blk + 1024, shape[0])
                ins = sy.dma_start(out=ap[blk:hi_, :], in_=src[blk:hi_, :])
                inc(c, ins, outst, 16)


def emit_attn(c, x_src, x_dst):
    io = c.io
    tag = "a"
    (xld, xlnst, xtld, wld, wgld, gq, yst, outst, vS, sS, pS, tS) = (
        c.sem(n) for n in ("xld", "xlnst", "xtld", "wld", "wgld", "gq",
                           "yst", "outst", "vS", "sS", "pS", "tS"))
    psA, psB, psC, psD = c.ps
    psK0, psK1 = c.psK
    Vsb = c.hsb
    lstg = c.ycomb
    WV, WK, WQ, WO = 0, 1, 0, 1
    c.bkrow = c.bfv(0, 1, 7424, 7936)

    emit_ln(c, tag, x_src, wait_tile="f0_xot{}")

    @sec(c, "gpsimd")
    def _(gp):
        for nm, slot in (("wvT", WV), ("wkT", WK)):
            for dcc in range(DC):
                ins = gp.dma_start(
                    out=c.warena[:, slot * 8192 + dcc * 1024:slot * 8192 + (dcc + 1) * 1024],
                    in_=io[nm][dcc * 128:(dcc + 1) * 128, :])
                inc(c, ins, wgld, 16, f"a_{nm}" if dcc == DC - 1 else None)
        ins = gp.dma_start(out=c.bkrow, in_=io["bk"])
        inc(c, ins, wgld, 16, "a_bkrow")
        wait_ev(c, gp, "gpsimd", "a_lev")
        ins = gp.dma_start(out=io["ccli"].rearrange("(h dk) dv -> dk h dv", h=NH),
                           in_=c.lsb.rearrange("dk (h dv) -> dk h dv", h=NH))
        inc(c, ins, gq, 16, "a_lwr")
        wait_ev(c, gp, "gpsimd", "a_lwr")
        ins = gp.collective_compute("AllGather", ALU.bypass, replica_groups=RG8,
                                    ins=[io["ccli"].opt()], outs=[io["cclo"].opt()])
        cc_inc(c, ins, "a_ccl")
        wait_ev(c, gp, "gpsimd", "a_ccl")
        for j in range(8):
            if j >= 2:
                wait_ev(c, gp, "gpsimd", f"a_s0acc{j - 2}")
            src = io["cclo"][j * 1024:(j + 1) * 1024, :] \
                .rearrange("(h dk) dv -> dk h dv", h=NH)
            ins = gp.dma_start(out=lstg[j % 2][0:64, :].rearrange("p (h dv) -> p h dv", h=NH),
                               in_=src)
            inc(c, ins, gq, 16)
            ins = gp.dma_start(out=lstg[j % 2][64:128, :].rearrange("p (h dv) -> p h dv", h=NH),
                               in_=src)
            inc(c, ins, gq, 16, f"a_lstg{j}")
        wait_ev(c, gp, "gpsimd", "a_vprojdone")
        for dcc in range(DC):
            ins = gp.dma_start(
                out=c.warena[:, WQ * 8192 + dcc * 1024:WQ * 8192 + (dcc + 1) * 1024],
                in_=io["wqT"][dcc * 128:(dcc + 1) * 128, :])
            inc(c, ins, wgld, 16, "a_wqT" if dcc == DC - 1 else None)
        wait_ev(c, gp, "gpsimd", "a_ktlast")
        for dcc in range(DC):
            ins = gp.dma_start(
                out=c.warena[:, WO * 8192 + dcc * 1024:WO * 8192 + (dcc + 1) * 1024],
                in_=io["woT"][dcc * 128:(dcc + 1) * 128, :])
            inc(c, ins, wgld, 16, "a_woT" if dcc == DC - 1 else None)

    @sec(c, "sync")
    def _(sy):
        wait_ev(c, sy, "sync", "a_xln")
        for dcc in range(DC):
            ins = sy.dma_start(out=c.xT[:, dcc * T:(dcc + 1) * T],
                               in_=io["xln_dram"][:, dcc * 128:(dcc + 1) * 128],
                               transpose=True)
            inc(c, ins, xtld, 16, "a_xT" if dcc == DC - 1 else None)
        wait_ev(c, sy, "sync", "a_xln")
        ins = sy.dma_start(out=c.lnx[0][:], in_=io["iotaq"][:, 0:1024])
        inc(c, ins, xtld, 16)
        ins = sy.dma_start(out=c.lnx[1][:], in_=io["iotaq"][:, 1024:2048])
        inc(c, ins, xtld, 16, "a_iotaq")
        for gp2 in range(8):
            wait_ev(c, sy, "sync", f"a_outhp{gp2}")
            ins = sy.dma_start(
                out=io["attn_dram"][:, gp2 * 128:(gp2 + 1) * 128]
                .rearrange("(ck p) d -> p ck d", p=128),
                in_=c.outhp[:].rearrange("p (ck d) -> p ck d", d=128))
            inc(c, ins, yst, 16, f"a_adw{gp2}")
        for ch in range(DC):
            wait_ev(c, sy, "sync", f"a_adw{ch}")
            if ch == 0:
                wait_ev(c, sy, "sync", "a_qtlast")
                wait_ev(c, sy, "sync", "a_ktlast")
            ins = sy.dma_start(out=c.xT[:, ch * T:(ch + 1) * T],
                               in_=io["attn_dram"][:, ch * 128:(ch + 1) * 128],
                               transpose=True)
            inc(c, ins, xtld, 16, f"a_ard{ch}")
        for i in range(NT):
            if i >= 2:
                wait_ev(c, sy, "sync", f"a_xo{i - 2}")
            ins = sy.dma_start(out=c.xtile[i % 2][:], in_=x_src[i * 128:(i + 1) * 128, :])
            inc(c, ins, xld, 16, f"a_cx{i}")

    @sec(c, "tensor")
    def _(te):
        wait_ev(c, te, "tensor", "a_xT")
        wait_ev(c, te, "tensor", "a_wvT")
        wait_ev(c, te, "tensor", "f0_yevvdone7")
        wait_ev(c, te, "tensor", "f0_geludone7")
        wait_ev(c, te, "tensor", "f0_throwev")
        for vt in range(NT):
            for nh in range(2):
                j = vt * 2 + nh
                pb = psC if j % 2 == 0 else psD
                if j >= 2:
                    wait_ev(c, te, "tensor", f"a_vev{j - 2}")
                for dcc in range(DC):
                    te.matmul(pb[:, 0:512],
                              c.xT[:, dcc * T + vt * 128:dcc * T + (vt + 1) * 128],
                              c.warena[:, WV * 8192 + dcc * 1024 + nh * 512:
                                        WV * 8192 + dcc * 1024 + (nh + 1) * 512],
                              start=(dcc == 0), stop=False)
                ins = te.matmul(pb[:, 0:512], c.ones_bf[0:1, 0:128],
                                c.bvsb[0:1, nh * 512:(nh + 1) * 512], start=False, stop=True)
                inc(c, ins, tS, 1, f"a_vmm{j}")
        alias_ev(c, "a_vprojdone", f"a_vmm{2 * NT - 1}")
        wait_ev(c, te, "tensor", "a_wkT")
        wait_ev(c, te, "tensor", "a_bkrow")
        first = True
        for ck in range(NCHUNK):
            for nh in range(2):
                pb = psA if nh == 0 else psB
                if ck >= 1:
                    wait_ev(c, te, "tensor", f"a_kcev{ck - 1}_{nh}")
                for dcc in range(DC):
                    te.matmul(pb[:, 0:512],
                              c.xT[:, dcc * T + ck * 128:dcc * T + (ck + 1) * 128],
                              c.warena[:, WK * 8192 + dcc * 1024 + nh * 512:
                                        WK * 8192 + dcc * 1024 + (nh + 1) * 512],
                              start=(dcc == 0), stop=False)
                ins = te.matmul(pb[:, 0:512], c.ones_bf[0:1, 0:128],
                                c.bfv(0, 1, c.BK0 + nh * 256, c.BK0 + (nh + 1) * 256), start=False, stop=True)
                inc(c, ins, tS, 1, f"a_kcmm{ck}_{nh}")
            wait_ev(c, te, "tensor", f"a_kcev{ck}_0")
            wait_ev(c, te, "tensor", f"a_kcev{ck}_1")
            for h in range(NH):
                ins = te.matmul(psK0[0:64, h * 64:(h + 1) * 64],
                                c.kc[:, h * 64:(h + 1) * 64],
                                Vsb[:, ck * 1024 + h * 64:ck * 1024 + (h + 1) * 64],
                                start=first, stop=False, skip_group_check=True)
                first = False
            if ck == NCHUNK - 1:
                inc(c, ins, tS, 1, "a_lmm")
        for half in range(2):
            wait_ev(c, te, "tensor", f"a_sthalf{half}")
            for ck in range(NCHUNK):
                pb = psA if ck % 2 == 0 else psB
                if ck >= 2:
                    wait_ev(c, te, "tensor", f"a_kc2ev{half}_{ck - 2}")
                for dcc in range(DC):
                    te.matmul(pb[:, 0:512],
                              c.xT[:, dcc * T + ck * 128:dcc * T + (ck + 1) * 128],
                              c.warena[:, WK * 8192 + dcc * 1024 + half * 512:
                                        WK * 8192 + dcc * 1024 + (half + 1) * 512],
                              start=(dcc == 0), stop=False)
                ins = te.matmul(pb[:, 0:512], c.ones_bf[0:1, 0:128],
                                c.bfv(0, 1, c.BK0 + half * 256, c.BK0 + (half + 1) * 256), start=False, stop=True)
                inc(c, ins, tS, 1, f"a_kc2mm{half}_{ck}")
                pk = c.psK[ck % 2]
                wait_ev(c, te, "tensor", f"a_kc2ev{half}_{ck}")
                if ck >= 2:
                    wait_ev(c, te, "tensor", f"a_stupd{half}_{ck - 2}")
                for hh in range(8):
                    hl, pr = divmod(hh, 2)
                    gh = half * 8 + hh
                    ins = te.matmul(pk[pr * 64:(pr + 1) * 64, hl * 64:(hl + 1) * 64],
                                    c.kc[:, hh * 64:(hh + 1) * 64],
                                    Vsb[:, ck * 1024 + gh * 64:ck * 1024 + (gh + 1) * 64],
                                    start=(hh == 0 or hh == 1), stop=False,
                                    skip_group_check=True,
                                    tile_position=(0, 64) if pr == 1 else None)
                inc(c, ins, tS, 1, f"a_kv{half}_{ck}")
            for hpl in range(4):
                gp2 = half * 4 + hpl
                wait_ev(c, te, "tensor", "a_wqT")
                if gp2 > 0:
                    wait_ev(c, te, "tensor", f"a_outhp{gp2 - 1}")
                for g in range(4):
                    pb = psC if g % 2 == 0 else psD
                    if g >= 2:
                        wait_ev(c, te, "tensor", f"a_qtev{gp2}_{g - 2}")
                    for dcc in range(DC):
                        ins = te.matmul(pb[:, 0:512],
                                        c.warena[:, WQ * 8192 + dcc * 1024 + gp2 * 128:
                                                  WQ * 8192 + dcc * 1024 + (gp2 + 1) * 128],
                                        c.xT[:, dcc * T + g * 512:dcc * T + (g + 1) * 512],
                                        start=(dcc == 0), stop=(dcc == DC - 1))
                    inc(c, ins, tS, 1, f"a_qtmm{gp2}_{g}")
                if gp2 == 7:
                    alias_ev(c, "a_qtlast", "a_qtmm7_3")
                for g in range(4):
                    pb = psC if g % 2 == 0 else psD
                    wait_ev(c, te, "tensor",
                            f"a_qtev{gp2}_{g + 2}" if g < 2 else f"a_ktev{gp2}_{g - 2}")
                    for dcc in range(DC):
                        ins = te.matmul(pb[:, 0:512],
                                        c.warena[:, WK * 8192 + dcc * 1024 + gp2 * 128:
                                                  WK * 8192 + dcc * 1024 + (gp2 + 1) * 128],
                                        c.xT[:, dcc * T + g * 512:dcc * T + (g + 1) * 512],
                                        start=(dcc == 0), stop=(dcc == DC - 1))
                    inc(c, ins, tS, 1, f"a_ktmm{gp2}_{g}")
                if gp2 == 7:
                    alias_ev(c, "a_ktlast", "a_ktmm7_3")
                if hpl == 0:
                    wait_ev(c, te, "tensor", f"a_kc2ev{half}_{NCHUNK - 1}")
                for cg in range(4):
                    for ci in range(4):
                        ck = cg * 4 + ci
                        g = ck // 4
                        wait_ev(c, te, "tensor", f"a_qtev{gp2}_{g}")
                        wait_ev(c, te, "tensor", f"a_ktev{gp2}_{g}")
                        for pr in range(2):
                            ja = ck * 2 + pr
                            pa = psA if ja % 2 == 0 else psB
                            if ja >= 2:
                                wait_ev(c, te, "tensor", f"a_amask{gp2}_{ja - 2}")
                            ins = te.matmul(pa[:, 0:128],
                                            c.bfv(pr * 64, (pr + 1) * 64, c.KT0 + ck * 64, c.KT0 + (ck + 1) * 64),
                                            c.bfv(pr * 64, (pr + 1) * 64, c.QT0 + ck * 64, c.QT0 + (ck + 1) * 64),
                                            start=True, stop=True)
                            inc(c, ins, tS, 1, f"a_amm{gp2}_{ja}")
                        po = psC if cg % 2 == 0 else psD
                        if ci == 0 and cg >= 2:
                            wait_ev(c, te, "tensor", f"a_outev{gp2}_{cg - 2}")
                        for pr in range(2):
                            ja = ck * 2 + pr
                            gh = gp2 * 2 + pr
                            wait_ev(c, te, "tensor", f"a_amask{gp2}_{ja}")
                            te.matmul(po[:, ci * 128 + pr * 64:ci * 128 + (pr + 1) * 64],
                                      c.asb[ja % 2][:],
                                      Vsb[:, ck * 1024 + gh * 64:ck * 1024 + (gh + 1) * 64],
                                      start=True, stop=False, skip_group_check=True)
                            ins = te.matmul(po[:, ci * 128 + pr * 64:ci * 128 + (pr + 1) * 64],
                                            c.bfv(pr * 64, (pr + 1) * 64, c.QT0 + ck * 64, c.QT0 + (ck + 1) * 64),
                                            c.bfv(pr * 64, (pr + 1) * 64,
                                                  ck * 128 + hpl * 32, ck * 128 + (hpl + 1) * 32),
                                            start=False, stop=True, skip_group_check=True)
                        inc(c, ins, tS, 1, f"a_omm{gp2}_{cg}" if ci == 3 else None)
        wait_ev(c, te, "tensor", "a_woT")
        for i in range(NT):
            for nh in range(2):
                j = i * 2 + nh
                pb = psA if j % 2 == 0 else psB
                if j >= 2:
                    wait_ev(c, te, "tensor", f"a_xaddh{j - 2}")
                for ch in range(DC):
                    wait_ev(c, te, "tensor", f"a_ard{ch}")
                    te.matmul(pb[:, 0:512],
                              c.xT[:, ch * T + i * 128:ch * T + (i + 1) * 128],
                              c.warena[:, WO * 8192 + ch * 1024 + nh * 512:
                                        WO * 8192 + ch * 1024 + (nh + 1) * 512],
                              start=(ch == 0), stop=False)
                ins = te.matmul(pb[:, 0:512], c.ones_bf[0:1, 0:128],
                                c.bosb[0:1, nh * 512:(nh + 1) * 512], start=False, stop=True)
                inc(c, ins, tS, 1, f"a_womm{j}")

    @sec(c, "vector")
    def _(ve):
        wait_ev(c, ve, "vector", "a_lmm")
        ins = ve.tensor_copy(out=c.lsb, in_=psK0[0:64, 0:1024])
        inc(c, ins, vS, 1, "a_lev")
        for j in range(8):
            wait_ev(c, ve, "vector", f"a_lstg{j}")
            if j == 0:
                ins = ve.tensor_scalar(out=c.s0sb, in0=lstg[0], scalar1=c.wprev_sb[:, 0:1],
                                       scalar2=None, op0=ALU.mult)
            else:
                ins = ve.scalar_tensor_tensor(out=c.s0sb, in0=lstg[j % 2],
                                              scalar=c.wprev_sb[:, j:j + 1], in1=c.s0sb,
                                              op0=ALU.mult, op1=ALU.add)
            inc(c, ins, vS, 1, f"a_s0acc{j}")
            wait_abs(c, ve, "vector", vS, vS.n)
        for half in range(2):
            sv = c.s0sb[:, half * 512:(half + 1) * 512]
            for pr in range(2):
                ins = ve.tensor_copy(
                    out=c.stt[pr * 64:(pr + 1) * 64, :],
                    in_=sv[pr * 64:(pr + 1) * 64, :]
                    .rearrange("p (hl pr2 dv) -> p hl pr2 dv", hl=4, pr2=2)[:, :, pr, :])
                inc(c, ins, vS, 1, f"a_sthalf{half}" if pr == 1 else None)
            wait_abs(c, ve, "vector", vS, vS.n)
            for ck in range(NCHUNK):
                ins = ve.tensor_copy(out=c.bfv(0, 128, ck * 128, (ck + 1) * 128), in_=c.stt)
                vf(c, ve, "vector", vS, ins)
                wait_ev(c, ve, "vector", f"a_kv{half}_{ck}")
                ins = ve.tensor_add(out=c.stt, in0=c.stt, in1=c.psK[ck % 2][:, 0:256])
                inc(c, ins, vS, 1, f"a_stupd{half}_{ck}")
                wait_abs(c, ve, "vector", vS, vS.n)
            for hpl in range(4):
                gp2 = half * 4 + hpl
                for h2 in range(2):
                    wait_ev(c, ve, "vector", "a_iotaq")
                    if gp2 > 0 or h2 > 0:
                        pgp2, ph2 = (gp2, 0) if h2 == 1 else (gp2 - 1, 1)
                        wait_ev(c, ve, "vector", f"a_gbfh{pgp2}_{ph2}")
                    ins = ve.tensor_scalar(out=c.scrA[:], in0=c.lnx[h2][:],
                                           scalar1=c.lngam[:, gp2:gp2 + 1], scalar2=None,
                                           op0=ALU.mult)
                    inc(c, ins, vS, 1, f"a_gps{gp2}_{h2}")
                for g in range(4):
                    wait_ev(c, ve, "vector", f"a_qtmm{gp2}_{g}")
                    wait_ev(c, ve, "vector", f"a_gbf{gp2}")
                    pb = psC if g % 2 == 0 else psD
                    ins = ve.scalar_tensor_tensor(
                        out=c.bfv(0, 128, c.QT0 + g * 256, c.QT0 + (g + 1) * 256), in0=pb[:, 0:512],
                        scalar=c.bqsb[:, gp2:gp2 + 1], in1=c.bfv(0, 128, c.GB0 + g * 256, c.GB0 + (g + 1) * 256),
                        op0=ALU.add, op1=ALU.mult)
                    inc(c, ins, vS, 1, f"a_qtev{gp2}_{g}")
                for cg in range(4):
                    for ci in range(4):
                        ck = cg * 4 + ci
                        for pr in range(2):
                            ja = ck * 2 + pr
                            wait_ev(c, ve, "vector", f"a_amm{gp2}_{ja}")
                            pa = psA if ja % 2 == 0 else psB
                            ins = ve.tensor_tensor(out=c.asb[ja % 2][:], in0=pa[:, 0:128],
                                                   in1=c.causalT[:], op=ALU.mult)
                            inc(c, ins, vS, 1, f"a_amask{gp2}_{ja}")
                    wait_ev(c, ve, "vector", f"a_omm{gp2}_{cg}")
                    if gp2 > 0 and cg == 0:
                        wait_ev(c, ve, "vector", f"a_adw{gp2 - 1}")
                    po = psC if cg % 2 == 0 else psD
                    ins = ve.tensor_copy(out=c.outhp[:, cg * 512:(cg + 1) * 512], in_=po[:, 0:512])
                    inc(c, ins, vS, 1, f"a_outev{gp2}_{cg}")
                alias_ev(c, f"a_outhp{gp2}", f"a_outev{gp2}_3")
        for i in range(NT):
            for nh in range(2):
                j = i * 2 + nh
                wait_ev(c, ve, "vector", f"a_womm{j}")
                wait_ev(c, ve, "vector", f"a_cx{i}")
                pb = psA if j % 2 == 0 else psB
                ins = ve.tensor_add(out=c.xtile[i % 2][:, nh * 512:(nh + 1) * 512],
                                    in0=c.xtile[i % 2][:, nh * 512:(nh + 1) * 512],
                                    in1=pb[:, 0:512])
                inc(c, ins, vS, 1, f"a_xaddh{j}")

    @sec(c, "scalar")
    def _(sc):
        for vt in range(NT):
            for nh in range(2):
                j = vt * 2 + nh
                wait_ev(c, sc, "scalar", f"a_vmm{j}")
                pb = psC if j % 2 == 0 else psD
                ins = sc.copy(out=Vsb[:, vt * 1024 + nh * 512:vt * 1024 + (nh + 1) * 512],
                              in_=pb[:, 0:512])
                inc(c, ins, sS, 1, f"a_vev{j}")
        for ck in range(NCHUNK):
            for nh in range(2):
                wait_ev(c, sc, "scalar", f"a_kcmm{ck}_{nh}")
                pb = psA if nh == 0 else psB
                ins = sc.copy(out=c.kc[:, nh * 512:(nh + 1) * 512], in_=pb[:, 0:512])
                inc(c, ins, sS, 1, f"a_kcev{ck}_{nh}")
        for half in range(2):
            for ck in range(NCHUNK):
                wait_ev(c, sc, "scalar", f"a_kc2mm{half}_{ck}")
                pb = psA if ck % 2 == 0 else psB
                ins = sc.copy(out=c.kc[:, 0:512], in_=pb[:, 0:512])
                inc(c, ins, sS, 1, f"a_kc2ev{half}_{ck}")
            for hpl in range(4):
                gp2 = half * 4 + hpl
                for h2 in range(2):
                    wait_ev(c, sc, "scalar", f"a_gps{gp2}_{h2}")
                    if gp2 > 0 and h2 == 0:
                        wait_ev(c, sc, "scalar", f"a_qtev{gp2 - 1}_3")
                    ins = sc.activation(out=c.bfv(0, 128, c.GB0 + h2 * 512, c.GB0 + (h2 + 1) * 512), in_=c.scrA[:],
                                        func=AF.Exp)
                    inc(c, ins, sS, 1, f"a_gbfh{gp2}_{h2}")
                    if h2 == 1:
                        alias_ev(c, f"a_gbf{gp2}", f"a_gbfh{gp2}_{h2}")
                for g in range(4):
                    wait_ev(c, sc, "scalar", f"a_ktmm{gp2}_{g}")
                    pb = psC if g % 2 == 0 else psD
                    ins = sc.activation(out=c.bfv(0, 128, c.KT0 + g * 256, c.KT0 + (g + 1) * 256), in_=pb[:, 0:512],
                                        func=AF.Identity, bias=c.bksb[:, gp2:gp2 + 1])
                    inc(c, ins, sS, 1, f"a_ktev{gp2}_{g}")
        for i in range(NT):
            wait_ev(c, sc, "scalar", f"a_xaddh{i * 2}")
            wait_ev(c, sc, "scalar", f"a_xaddh{i * 2 + 1}")
            ins = sc.dma_start(out=x_dst[i * 128:(i + 1) * 128, :], in_=c.xtile[i % 2][:])
            inc(c, ins, outst, 16, f"a_xo{i}" if i < NT - 1 else "a_done")
            alias_ev(c, f"a_xot{i}", f"a_xo{i}" if i < NT - 1 else "a_done")

    if c.debug:
        emit_attn_debug(c)


def emit_copy(c, src, dst, wait_before):
    outst = c.sem("outst")

    @sec(c, "sync")
    def _(sy):
        wait_ev(c, sy, "sync", wait_before)
        for i in range(NT):
            ins = sy.dma_start(out=dst[i * 128:(i + 1) * 128, :], in_=src[i * 128:(i + 1) * 128, :])
            inc(c, ins, outst, 16, "copy_done" if i == NT - 1 else None)


def emit_final_fence(c):
    outst = c.sem("outst")

    @sec(c, "sync")
    def _(sy):
        wait_abs(c, sy, "sync", outst, outst.n)


DEBUG_TAPS = ()
LAST_RESULTS = None


def kernel(**inputs):
    global LAST_RESULTS
    n = 8
    x = np.asarray(inputs["L0_attn_x"], dtype=np.float32).reshape(2 * 8192, D)
    nc, c = build_program(debug=DEBUG_TAPS)
    f32 = lambda a: np.ascontiguousarray(np.asarray(a), dtype=np.float32)
    wqT = f32(np.asarray(inputs["L0_attn_Wq"]).T)
    wkT = f32(np.asarray(inputs["L0_attn_Wk"]).T)
    wvT = f32(np.asarray(inputs["L0_attn_Wv"]).T)
    woT = f32(np.asarray(inputs["L0_attn_Wo"]).T)
    base = {
        "wqT": wqT, "wkT": wkT, "wvT": wvT, "woT": woT,
        "bq": f32(inputs["L0_attn_bq"]).reshape(1, D),
        "bk": f32(inputs["L0_attn_bk"]).reshape(1, D),
        "bv": f32(inputs["L0_attn_bv"]).reshape(1, D),
        "bo": f32(inputs["L0_attn_bo"]).reshape(1, D),
        "gamma": f32(inputs["L0_attn_gamma"]).reshape(NH, 1),
        "rw": f32(inputs["L0_ffn_router_w"]),
        "rb": f32(inputs["L0_ffn_router_b"]).reshape(1, E),
        "w1": f32(inputs["L0_ffn_w1"]),
        "b1": f32(inputs["L0_ffn_b1"]),
        "w2": f32(inputs["L0_ffn_w2"]),
        "b2": f32(inputs["L0_ffn_b2"]).reshape(1, E * D),
        "lng": f32(inputs["L0_ffn_ln_g"]).reshape(1, D),
        "lnb": f32(inputs["L0_ffn_ln_b"]).reshape(1, D),
    }
    in_maps = []
    for cc_ in range(n):
        seq_off = (cc_ % 4) * T
        iq = np.tile(np.arange(seq_off, seq_off + T, dtype=np.float32), (128, 1))
        wp = np.zeros((128, 8), np.float32)
        grp = range(4 * (cc_ // 4), cc_)
        for j in grp:
            wp[:, j] = 1.0
        m = dict(base)
        m["x_in"] = np.ascontiguousarray(x[cc_ * T:(cc_ + 1) * T])
        m["iotaq"] = iq
        m["wprev"] = wp
        in_maps.append(m)
    res = run_bass_kernel_spmd(nc, in_maps, core_ids=list(range(n)))
    LAST_RESULTS = res.results
    out = np.concatenate([res.results[cc_]["y_out"] for cc_ in range(n)], axis=0)
    return out.reshape(2, 8192, D)
